# version 17
# baseline (speedup 1.0000x reference)
import numpy as np
import ml_dtypes
import concourse.bass as bass
import concourse.mybir as mybir
from concourse.bass_utils import run_bass_kernel_spmd

F32 = mybir.dt.float32
BF16 = mybir.dt.bfloat16
AF = mybir.ActivationFunctionType
ALU = mybir.AluOpType
BFNP = ml_dtypes.bfloat16

ST = 512
NST = 8
NDC = 8
CSH = 16.0
NEG = -1.0e30

_DEBUG = False
_PHASES = 3
_P2N = NST
_P2NOMASK = False
_P2NOSUM = False
_P2NOO = False
_P2NOEPI = False
_P3N = NST

ENGS = ("sync", "gpsimd", "scalar", "vector", "tensor")
SEMS = ("ld", "st", "out", "pe", "act", "dve", "pool")




class Bld:

    def __init__(self):
        self.ops = {e: [] for e in ENGS}
        self.sems = {}
        self.cnt = {s: 0 for s in SEMS}
        self.ev = {}

    def op(self, eng, fn):
        self.ops[eng].append(("call", fn))

    def mark(self, eng, sem, inc, names, fn):
        self.cnt[sem] += inc
        val = self.cnt[sem]
        if isinstance(names, str):
            names = [names]
        for n in names:
            assert n not in self.ev, n
            self.ev[n] = (sem, val, eng)
        self.ops[eng].append(
            ("call", lambda e, fn=fn, sem=sem, inc=inc: fn(e).then_inc(self.sems[sem], inc), sem, inc, names[0]))

    def alias(self, newname, name):
        self.ev[newname] = self.ev[name]

    def wait(self, eng, name):
        self.ops[eng].append(("wait", name))

    def waitval(self, eng, sem, val):
        self.ops[eng].append(("waitval", sem, val))

    def finalize(self):
        final = {}
        for eng in ENGS:
            out = []
            lastw = {}
            for item in self.ops[eng]:
                if item[0] == "call":
                    out.append(item[1])
                    continue
                if item[0] == "wait":
                    if item[1] not in self.ev:
                        if _PHASES < 3 or _P2N < NST or _P2NOMASK or _P2NOSUM or _P2NOO or _P2NOEPI or _P3N < NST:
                            continue
                        raise KeyError(item[1])
                    sem, val, prod = self.ev[item[1]]
                    if prod == eng:
                        continue
                else:
                    sem, val = item[1], item[2]
                if lastw.get(sem, 0) >= val:
                    continue
                lastw[sem] = val
                out.append(lambda e, sem=sem, val=val: e.wait_ge(self.sems[sem], val))
            final[eng] = out
        return final


def build_nc():
    nc = bass.Bass("TRN2", target_bir_lowering=False)

    xin = nc.dram_tensor("xin", [NST, 128, NDC, ST], F32, kind="ExternalInput")
    wq_h = nc.dram_tensor("wq_h", [128, 8, 1024], F32, kind="ExternalInput")
    wk_h = nc.dram_tensor("wk_h", [128, 8, 1024], F32, kind="ExternalInput")
    wv_h = nc.dram_tensor("wv_h", [128, 8, 1024], F32, kind="ExternalInput")
    u1_h = nc.dram_tensor("u1_h", [128, 8, 256], F32, kind="ExternalInput")
    v1_h = nc.dram_tensor("v1_h", [128, 2, 4096], F32, kind="ExternalInput")
    u2_h = nc.dram_tensor("u2_h", [128, 32, 256], F32, kind="ExternalInput")
    v2_h = nc.dram_tensor("v2_h", [128, 2, 1024], F32, kind="ExternalInput")
    b1_h = nc.dram_tensor("b1_h", [128, 32], F32, kind="ExternalInput")
    b2_h = nc.dram_tensor("b2_h", [128, 8], F32, kind="ExternalInput")
    mk_h = nc.dram_tensor("mk_h", [128, 128], BF16, kind="ExternalInput")
    out_h = nc.dram_tensor("out_h", [NST, 128, NDC, ST], F32, kind="ExternalOutput")

    skind = "ExternalOutput" if _DEBUG else "Internal"
    qscr = nc.dram_tensor("qscr", [128, NST, 8, ST], BF16, kind=skind)
    kscr = nc.dram_tensor("kscr", [128, 32, 8, 128], BF16, kind=skind)
    vscr = nc.dram_tensor("vscr", [128, 32, 1024], BF16, kind=skind)
    yscr = nc.dram_tensor("yscr", [128, NST, NDC, ST], F32, kind=skind)
    ybscr = nc.dram_tensor("ybscr", [128, NST, NDC, ST], BF16, kind=skind)

    wq = nc.alloc_sbuf_tensor("wq", [128, 8192], BF16)
    wk = nc.alloc_sbuf_tensor("wk", [128, 8192], BF16)
    wv = nc.alloc_sbuf_tensor("wv", [128, 8192], BF16)
    pt = nc.alloc_sbuf_tensor("pt", [128, 16384], BF16)
    qsm = nc.alloc_sbuf_tensor("qsm", [128, 1024], BF16)
    bf2 = nc.alloc_sbuf_tensor("bf2", [128, 4096], BF16)
    bf3 = nc.alloc_sbuf_tensor("bf3", [128, 4096], BF16)
    bf4 = nc.alloc_sbuf_tensor("bf4", [128, 4096], BF16)
    bf5 = nc.alloc_sbuf_tensor("bf5", [128, 4096], BF16)
    xbuf = [nc.alloc_sbuf_tensor("xbuf0", [128, 4096], F32),
            nc.alloc_sbuf_tensor("xbuf1", [128, 4096], F32)]
    masks = nc.alloc_sbuf_tensor("masks", [128, 128], BF16)
    rows = nc.alloc_sbuf_tensor("rows", [128, 2560], F32)
    rbf = nc.alloc_sbuf_tensor("rbf", [128, 2048], BF16)
    tmp32 = nc.alloc_sbuf_tensor("tmp32", [128, 2048], F32)
    ynt = nc.alloc_sbuf_tensor("ynt", [128, 1024], F32)
    rb = nc.alloc_sbuf_tensor("rb", [128, 512], F32)
    mrsb = nc.alloc_sbuf_tensor("mrsb", [128, 512], F32)
    ystage = nc.alloc_sbuf_tensor("ystage", [128, 1024], F32)
    qld = nc.alloc_sbuf_tensor("qld", [128, 4096], BF16)
    ybuf2 = nc.alloc_sbuf_tensor("ybuf2", [128, 4096], F32)
    p_sb = nc.alloc_sbuf_tensor("p_sb", [128, 1024], BF16)
    q2 = qsm
    b1s = nc.alloc_sbuf_tensor("b1s", [128, 32], F32)
    b2s = nc.alloc_sbuf_tensor("b2s", [128, 8], F32)
    onesrow = nc.alloc_sbuf_tensor("onesrow", [1, 128], BF16)
    statcol = nc.alloc_sbuf_tensor("statcol", [128, 2], BF16)
    cm16 = nc.alloc_sbuf_tensor("cm16", [128, 1], F32)
    ceps = nc.alloc_sbuf_tensor("ceps", [128, 1], F32)

    pb = [nc.alloc_psum_tensor(f"pb{k}", [128, 512], F32) for k in range(8)]

    B = Bld()
    for s in SEMS:
        B.sems[s] = nc.alloc_semaphore("sem_" + s)

    MEAN, MSQ, LNV, VAR, MR, RSTD = 0, 512, 1024, 1536, 2048, 512
    RECIP, RECIP2 = 512, 1024
    RH, RL, MH, ML = 0, 512, 1024, 1536
    SH, SL = 0, 512

    B.mark("sync", "ld", 16, "ld_masks",
           lambda e: e.dma_start(out=masks[:], in_=mk_h.ap()))
    B.mark("sync", "ld", 16, "ld_b1", lambda e: e.dma_start(out=b1s[:], in_=b1_h.ap()))
    B.mark("sync", "ld", 16, "ld_b2", lambda e: e.dma_start(out=b2s[:], in_=b2_h.ap()))

    B.mark("gpsimd", "pool", 1, "cst_ones", lambda e: e.memset(onesrow[:], 1.0))
    B.mark("gpsimd", "pool", 1, "cst_s0", lambda e: e.memset(statcol[:, 0:1], 1.0 / 1024.0))
    B.mark("gpsimd", "pool", 1, "cst_s1", lambda e: e.memset(statcol[:, 1:2], 1.0))
    B.mark("gpsimd", "pool", 1, "cst_m16", lambda e: e.memset(cm16[:], -CSH))
    B.mark("gpsimd", "pool", 1, "cst_done", lambda e: e.memset(ceps[:], 1e-5))
    B.mark("gpsimd", "st", 16, "w_q",
           lambda e: e.dma_start(out=wq[:].rearrange("p (c n) -> p c n", c=8), in_=wq_h.ap()))
    B.mark("gpsimd", "st", 16, "w_k",
           lambda e: e.dma_start(out=wk[:].rearrange("p (c n) -> p c n", c=8), in_=wk_h.ap()))
    B.mark("gpsimd", "st", 16, "w_v",
           lambda e: e.dma_start(out=wv[:].rearrange("p (c n) -> p c n", c=8), in_=wv_h.ap()))

    def xb_ap():
        return pt[:, 0:4096]

    def sq_ap():
        return pt[:, 4096:8192]

    def xn_ap(i):
        return pt[:, 8192 + (i % 2) * 4096: 12288 + (i % 2) * 4096]

    def yn_ap(i):
        return pt[:, 8192 + (i % 2) * 4096: 12288 + (i % 2) * 4096]

    def ln_rows_actA(ph, i):
        B.wait("scalar", f"{ph}_stats{i}")
        B.op("scalar", lambda e: e.activation(out=rows[0:1, MEAN:MEAN + 512], in_=pb[0][0:1, :], func=AF.Copy))
        B.op("scalar", lambda e: e.activation(out=rows[0:1, MSQ:MSQ + 512], in_=pb[1][0:1, :], func=AF.Copy))
        B.mark("scalar", "act", 1, f"{ph}_rowsA{i}",
               lambda e: e.activation(out=rows[0:1, LNV:LNV + 512], in_=rows[0:1, MEAN:MEAN + 512], func=AF.Square))

    def ln_rows_actB(ph, i):
        B.wait("scalar", f"{ph}_var{i}")
        B.op("scalar", lambda e: e.activation(out=rows[0:1, LNV:LNV + 512], in_=rows[0:1, VAR:VAR + 512],
                                              func=AF.Ln, bias=ceps[0:1, :]))
        B.op("scalar", lambda e: e.activation(out=rows[0:1, RSTD:RSTD + 512], in_=rows[0:1, LNV:LNV + 512],
                                              func=AF.Exp, scale=-0.5))
        B.mark("scalar", "act", 1, f"{ph}_rowsB{i}",
               lambda e: e.activation(out=rbf[0:1, RH:RH + 512], in_=rows[0:1, RSTD:RSTD + 512], func=AF.Copy))

    def ln_rows_actC(ph, i):
        B.wait("scalar", f"{ph}_mr{i}")
        B.mark("scalar", "act", 1, f"{ph}_rowsC{i}",
               lambda e: e.activation(out=rbf[0:1, MH:MH + 512], in_=rows[0:1, MR:MR + 512], func=AF.Copy))

    def ln_rows_act(ph, i):
        B.wait("scalar", f"{ph}_stats{i}")
        B.op("scalar", lambda e: e.activation(out=rows[0:1, MEAN:MEAN + 512], in_=pb[0][0:1, :], func=AF.Copy))
        B.op("scalar", lambda e: e.activation(out=rows[0:1, MSQ:MSQ + 512], in_=pb[1][0:1, :], func=AF.Copy))
        B.mark("scalar", "act", 1, f"{ph}_rowsA{i}",
               lambda e: e.activation(out=rows[0:1, LNV:LNV + 512], in_=rows[0:1, MEAN:MEAN + 512], func=AF.Square))
        B.wait("scalar", f"{ph}_var{i}")
        B.op("scalar", lambda e: e.activation(out=rows[0:1, LNV:LNV + 512], in_=rows[0:1, VAR:VAR + 512],
                                              func=AF.Ln, bias=ceps[0:1, :]))
        B.op("scalar", lambda e: e.activation(out=rows[0:1, RSTD:RSTD + 512], in_=rows[0:1, LNV:LNV + 512],
                                              func=AF.Exp, scale=-0.5))
        B.mark("scalar", "act", 1, f"{ph}_rowsB{i}",
               lambda e: e.activation(out=rbf[0:1, RH:RH + 512], in_=rows[0:1, RSTD:RSTD + 512], func=AF.Copy))
        B.wait("scalar", f"{ph}_mr{i}")
        B.mark("scalar", "act", 1, f"{ph}_rowsC{i}",
               lambda e: e.activation(out=rbf[0:1, MH:MH + 512], in_=rows[0:1, MR:MR + 512], func=AF.Copy))

    def ln_rows_dve(ph, i):
        B.wait("vector", f"{ph}_rowsA{i}")
        B.mark("vector", "dve", 1, f"{ph}_var{i}",
               lambda e: e.tensor_tensor(out=rows[0:1, VAR:VAR + 512], in0=rows[0:1, MSQ:MSQ + 512],
                                         in1=rows[0:1, LNV:LNV + 512], op=ALU.subtract))
        B.wait("vector", f"{ph}_rowsB{i}")
        B.op("vector", lambda e: e.tensor_tensor(out=rows[0:1, MR:MR + 512], in0=rows[0:1, MEAN:MEAN + 512],
                                                 in1=rows[0:1, RSTD:RSTD + 512], op=ALU.mult))
        B.mark("vector", "dve", 1, f"{ph}_mr{i}",
               lambda e: e.tensor_tensor(out=rbf[0:1, RL:RL + 512], in0=rows[0:1, RSTD:RSTD + 512],
                                         in1=rbf[0:1, RH:RH + 512], op=ALU.subtract))
        B.wait("vector", f"{ph}_rowsC{i}")
        B.mark("vector", "dve", 1, f"{ph}_ml{i}",
               lambda e: e.tensor_tensor(out=rbf[0:1, ML:ML + 512], in0=rows[0:1, MR:MR + 512],
                                         in1=rbf[0:1, MH:MH + 512], op=ALU.subtract))

    def pe_stats(ph, i, src_xb, src_sq):
        B.wait("tensor", f"{ph}_xb{i}")
        B.wait("tensor", f"{ph}_sq{i}")
        if i >= 1:
            B.wait("tensor", f"{ph}_rowsA{i-1}")
        B.wait("tensor", "cst_done")
        for c in range(NDC):
            B.op("tensor", lambda e, c=c: e.matmul(out=pb[0][0:1, :], lhsT=statcol[:, 0:1],
                                                   rhs=src_xb()[:, c * 512:(c + 1) * 512],
                                                   start=(c == 0), stop=(c == NDC - 1)))
        for c in range(NDC):
            fn = lambda e, c=c: e.matmul(out=pb[1][0:1, :], lhsT=statcol[:, 0:1],
                                         rhs=src_sq()[:, c * 512:(c + 1) * 512],
                                         start=(c == 0), stop=(c == NDC - 1))
            if c == NDC - 1:
                B.mark("tensor", "pe", 1, f"{ph}_stats{i}", fn)
            else:
                B.op("tensor", fn)

    def pe_bcast(ph, i, consumer_ev):
        B.wait("tensor", f"{ph}_ml{i}")
        if consumer_ev is not None:
            B.wait("tensor", consumer_ev)
        B.op("tensor", lambda e: e.matmul(out=pb[2][:], lhsT=onesrow[0:1, :], rhs=rbf[0:1, RH:RH + 512], start=True, stop=False))
        B.op("tensor", lambda e: e.matmul(out=pb[2][:], lhsT=onesrow[0:1, :], rhs=rbf[0:1, RL:RL + 512], start=False, stop=True))
        B.op("tensor", lambda e: e.matmul(out=pb[3][:], lhsT=onesrow[0:1, :], rhs=rbf[0:1, MH:MH + 512], start=True, stop=False))
        B.mark("tensor", "pe", 1, f"{ph}_bcast{i}",
               lambda e: e.matmul(out=pb[3][:], lhsT=onesrow[0:1, :], rhs=rbf[0:1, ML:ML + 512], start=False, stop=True))

    qkv_banks = []

    def p1_dve_pre(i):
        B.wait("vector", f"p1_x{i}")
        if i >= 1:
            B.wait("vector", f"p1_stats{i-1}")
        B.mark("vector", "dve", 1, f"p1_xb{i}",
               lambda e, i=i: e.tensor_copy(out=xb_ap(), in_=xbuf[i % 2][:]))
        B.mark("vector", "dve", 1, f"p1_sq{i}",
               lambda e: e.tensor_tensor(out=sq_ap(), in0=xb_ap(), in1=xb_ap(), op=ALU.mult))

    def p1_dve_xn(i):
        B.wait("vector", f"p1_bcast{i}")
        if i >= 2:
            B.wait("vector", f"p1_qkv{i-2}")
        for c in range(NDC):
            B.op("vector", lambda e, c=c, i=i: e.tensor_tensor(
                out=tmp32[:, (c % 2) * 512:(c % 2) * 512 + 512],
                in0=xbuf[i % 2][:, c * 512:(c + 1) * 512], in1=pb[2][:], op=ALU.mult))
            fn = lambda e, c=c, i=i: e.tensor_tensor(
                out=xn_ap(i)[:, c * 512:(c + 1) * 512],
                in0=tmp32[:, (c % 2) * 512:(c % 2) * 512 + 512], in1=pb[3][:], op=ALU.subtract)
            if c == NDC - 1:
                B.mark("vector", "dve", 1, f"p1_xn{i}", fn)
            else:
                B.op("vector", fn)

    def p1_pe_qkv(i):
        B.wait("tensor", f"p1_xn{i}")
        for kind, w in (("q", wq), ("k", wk)):
            B.wait("tensor", "w_" + kind)
            for ec in range(NDC):
                if len(qkv_banks) >= 4:
                    B.wait("tensor", qkv_banks[-4])
                for dc in range(NDC):
                    fn = lambda e, dc=dc, ec=ec, i=i, w=w, bank=pb[4 + len(qkv_banks) % 4]: e.matmul(
                        out=bank[:], lhsT=w[:, dc * 1024 + ec * 128: dc * 1024 + ec * 128 + 128],
                        rhs=xn_ap(i)[:, dc * 512:(dc + 1) * 512],
                        start=(dc == 0), stop=(dc == NDC - 1))
                    if dc == NDC - 1:
                        B.mark("tensor", "pe", 1, f"p1_mm{kind}{i}_{ec}", fn)
                    else:
                        B.op("tensor", fn)
                qkv_banks.append(f"p1_ev{kind}{i}_{ec}")
        B.wait("tensor", "w_v")
        for k in range(4):
            for h in range(2):
                if len(qkv_banks) >= 4:
                    B.wait("tensor", qkv_banks[-4])
                for dc in range(NDC):
                    fn = lambda e, dc=dc, k=k, h=h, i=i, bank=pb[4 + len(qkv_banks) % 4]: e.matmul(
                        out=bank[:], lhsT=xn_ap(i)[:, dc * 512 + k * 128: dc * 512 + k * 128 + 128],
                        rhs=wv[:, dc * 1024 + h * 512: dc * 1024 + h * 512 + 512],
                        start=(dc == 0), stop=(dc == NDC - 1))
                    if dc == NDC - 1:
                        B.mark("tensor", "pe", 1, f"p1_mmv{i}_{k}_{h}", fn)
                    else:
                        B.op("tensor", fn)
                qkv_banks.append(f"p1_evv{i}_{k}_{h}")
        B.alias(f"p1_qkv{i}", f"p1_mmv{i}_3_1")

    def p1_act_evacs(i):
        for ec in range(NDC):
            B.wait("scalar", f"p1_mmq{i}_{ec}")
            if ec >= 2:
                B.wait("scalar", f"p1_qdma{i}_{ec-2}")
            elif i >= 1:
                B.wait("scalar", f"p1_qdma{i-1}_{ec+6}")
            B.mark("scalar", "act", 1, f"p1_evq{i}_{ec}",
                   lambda e, ec=ec, bank=pb[4 + ec % 4]: e.activation(
                       out=qsm[:, (ec % 2) * 512:(ec % 2) * 512 + 512], in_=bank[:], func=AF.Copy, scale=1.0 / 32.0))
        if i >= 1:
            B.wait("scalar", f"p1_kdma{i-1}")
        kv = bf2[:].rearrange("p (s e t) -> p s e t", s=4, e=8)
        for ec in range(NDC):
            B.wait("scalar", f"p1_mmk{i}_{ec}")
            B.mark("scalar", "act", 1, f"p1_evk{i}_{ec}",
                   lambda e, ec=ec, kv=kv, bank=pb[4 + (8 + ec) % 4]: e.activation(
                       out=kv[:, :, ec, :], in_=bank[:], func=AF.Copy))
        if i >= 1:
            B.wait("scalar", f"p1_vdma{i-1}")
        for k in range(4):
            for h in range(2):
                B.wait("scalar", f"p1_mmv{i}_{k}_{h}")
                B.mark("scalar", "act", 1, f"p1_evv{i}_{k}_{h}",
                       lambda e, k=k, h=h, bank=pb[4 + (16 + k * 2 + h) % 4]: e.activation(
                           out=bf3[:, k * 1024 + h * 512: k * 1024 + h * 512 + 512], in_=bank[:], func=AF.Copy))

    def p1_gp_dmas(i):
        for ec in range(NDC):
            B.wait("gpsimd", f"p1_evq{i}_{ec}")
            B.mark("gpsimd", "st", 16, f"p1_qdma{i}_{ec}",
                   lambda e, i=i, ec=ec: e.dma_start(out=qscr.ap()[:, i, ec],
                                                     in_=qsm[:, (ec % 2) * 512:(ec % 2) * 512 + 512]))
        B.wait("gpsimd", f"p1_evk{i}_{NDC-1}")
        B.mark("gpsimd", "st", 16, f"p1_kdma{i}",
               lambda e, i=i: e.dma_start(out=kscr.ap()[:, 4 * i:4 * i + 4],
                                          in_=bf2[:].rearrange("p (s e t) -> p s e t", s=4, e=8)))
        B.wait("gpsimd", f"p1_evv{i}_3_1")
        B.mark("gpsimd", "st", 16, f"p1_vdma{i}",
               lambda e, i=i: e.dma_start(out=vscr.ap()[:, 4 * i:4 * i + 4],
                                          in_=bf3[:].rearrange("p (s d) -> p s d", s=4)))

    for i in range(NST):
        if i >= 2:
            B.wait("sync", f"p1_xn{i-2}")
        B.mark("sync", "ld", 16, f"p1_x{i}",
               lambda e, i=i: e.dma_start(out=xbuf[i % 2][:].rearrange("p (c t) -> p c t", c=8),
                                          in_=xin.ap()[i]))

    ln_rows_act("p1", 0)
    ln_rows_act("p1", 1)
    for i in range(NST):
        if i + 2 < NST:
            ln_rows_act("p1", i + 2)
        p1_act_evacs(i)

    p1_dve_pre(0)
    ln_rows_dve("p1", 0)
    p1_dve_pre(1)
    ln_rows_dve("p1", 1)
    p1_dve_xn(0)
    p1_dve_pre(2)
    for i in range(NST):
        if i + 2 < NST:
            ln_rows_dve("p1", i + 2)
        if i + 1 < NST:
            p1_dve_xn(i + 1)
        if i + 3 < NST:
            p1_dve_pre(i + 3)

    for i in range(NST):
        p1_gp_dmas(i)

    pe_stats("p1", 0, xb_ap, sq_ap)
    pe_bcast("p1", 0, None)
    pe_stats("p1", 1, xb_ap, sq_ap)
    for i in range(NST):
        if i + 1 < NST:
            pe_bcast("p1", i + 1, f"p1_xn{i}" if i >= 0 else None)
        if i + 2 < NST:
            pe_stats("p1", i + 2, xb_ap, sq_ap)
        p1_pe_qkv(i)

    p1_act_end = B.cnt["act"]
    p1_dve_end = B.cnt["dve"]
    p1_pe_end = B.cnt["pe"]

    B.wait("gpsimd", f"p1_qkv{NST-1}")
    B.mark("gpsimd", "st", 16, "ffn_u2",
           lambda e: e.dma_start(out=wq[:].rearrange("p (c n) -> p c n", c=32), in_=u2_h.ap()))
    B.mark("gpsimd", "st", 16, "ffn_v1",
           lambda e: e.dma_start(out=wk[:].rearrange("p (c n) -> p c n", c=2), in_=v1_h.ap()))
    B.mark("gpsimd", "st", 16, "ffn_u1",
           lambda e: e.dma_start(out=wv[:, 0:2048].rearrange("p (c n) -> p c n", c=8), in_=u1_h.ap()))
    B.mark("gpsimd", "st", 16, "ffn_v2",
           lambda e: e.dma_start(out=wv[:, 2048:4096].rearrange("p (c n) -> p c n", c=2), in_=v2_h.ap()))

    grp_sched = []
    for i in range(NST):
        for g in range(i + 1):
            grp_sched.append((i, g))

    def p2_sync(i):
        B.wait("sync", f"p1_qdma{i}_{NDC-1}")
        if i >= 1:
            B.wait("sync", f"p2_S{i-1}_{4*i-1}")
        B.mark("sync", "ld", 16, f"p2_q{i}",
               lambda e, i=i: e.dma_start(out=qld[:].rearrange("p (c t) -> p c t", c=8), in_=qscr.ap()[:, i]))
        for g in range(i + 1):
            gg = grp_sched.index((i, g))
            B.wait("sync", f"p1_kdma{g}")
            if gg >= 2:
                B.wait("sync", f"p2_Sg{gg-2}")
            B.mark("sync", "ld", 16, f"p2_kld{gg}",
                   lambda e, g=g, kld=(bf4 if gg % 2 == 0 else bf5): e.dma_start(
                       out=kld[:].rearrange("p (s e t) -> p s e t", s=4, e=8),
                       in_=kscr.ap()[:, 4 * g:4 * g + 4]))
        if not _P2NOO:
            if i >= 2:
                B.wait("sync", f"p2_epidone{i-2}")
            else:
                B.wait("sync", f"p1_xn{NST-2+i}")
            B.mark("sync", "ld", 16, f"p2_x{i}",
               lambda e, i=i: e.dma_start(out=xbuf[i % 2][:].rearrange("p (c t) -> p c t", c=8), in_=xin.ap()[i]))
        for g in range(i + 1):
            gg = grp_sched.index((i, g))
            if _P2NOO:
                continue
            B.wait("sync", f"p1_vdma{g}")
            B.wait("sync", f"p1_kdma{NST-1}" if gg % 2 == 0 else f"p1_vdma{NST-1}")
            if gg >= 2:
                B.wait("sync", f"p2_Og{gg-2}")
            B.mark("sync", "ld", 16, f"p2_vld{gg}",
                   lambda e, g=g, vld=(bf2 if gg % 2 == 0 else bf3): e.dma_start(
                       out=vld[:].rearrange("p (s d) -> p s d", s=4),
                       in_=vscr.ap()[:, 4 * g:4 * g + 4]))

    OPERM = [1, 2, 4, 5, 6, 7, 0, 3]

    def p2_pe(i):
        n = 4 * (i + 1)
        B.wait("tensor", f"p2_q{i}")
        for sc in range(n):
            gg = grp_sched.index((i, sc // 4))
            kld = bf4 if gg % 2 == 0 else bf5
            B.wait("tensor", f"p2_kld{gg}")
            if sc >= 2:
                B.wait("tensor", f"p2_exp{i}_{sc-2}")
            elif i >= 1:
                B.wait("tensor", f"p2_obank{i-1}_{sc}")
            bank = pb[1 + sc % 2]
            q0 = 128 * (sc - (n - 4)) if sc >= n - 4 else 0
            for ec in range(NDC):
                fn = lambda e, sc=sc, ec=ec, kld=kld, bank=bank, q0=q0: e.matmul(
                    out=bank[:, q0:512], lhsT=kld[:, (sc % 4) * 1024 + ec * 128:(sc % 4) * 1024 + ec * 128 + 128],
                    rhs=qld[:, ec * 512 + q0:(ec + 1) * 512],
                    start=(ec == 0), stop=(ec == NDC - 1))
                if ec == NDC - 1:
                    names = [f"p2_S{i}_{sc}"]
                    if sc % 4 == 3:
                        names.append(f"p2_Sg{gg}")
                    B.mark("tensor", "pe", 1, names, fn)
                else:
                    B.op("tensor", fn)
            if sc >= 1 and not _P2NOSUM:
                scp = sc - 1
                qp = 128 * (scp - (n - 4)) if scp >= n - 4 else 0
                B.wait("tensor", f"p2_exp{i}_{scp}")
                if scp == 0 and i >= 1:
                    B.wait("tensor", f"p2_obank{i-1}_6")
                B.op("tensor", lambda e, scp=scp, n=n, qp=qp: e.matmul(
                    out=pb[0][0:1, qp:512], lhsT=statcol[:, 1:2], rhs=pt[:, scp * 512 + qp:(scp + 1) * 512],
                    start=(scp == 0), stop=(scp == n - 1)))
        if not _P2NOSUM:
            B.wait("tensor", f"p2_exp{i}_{n-1}")
            B.mark("tensor", "pe", 1, f"p2_sumdone{i}",
                   lambda e, n=n: e.matmul(out=pb[0][0:1, 384:512], lhsT=statcol[:, 1:2],
                                           rhs=pt[:, (n - 1) * 512 + 384: n * 512], start=False, stop=True))

        def emit_rbmm():
            if _P2NOSUM:
                return
            B.wait("tensor", f"p2_recipA{i}")
            B.wait("tensor", f"p2_sl{i}")
            if i >= 1:
                B.wait("tensor", f"p2_obank{i-1}_7")
            B.op("tensor", lambda e: e.matmul(out=pb[3][:], lhsT=onesrow[0:1, :], rhs=rbf[0:1, SH:SH + 512], start=True, stop=False))
            B.mark("tensor", "pe", 1, f"p2_rbmm{i}",
                   lambda e: e.matmul(out=pb[3][:], lhsT=onesrow[0:1, :], rhs=rbf[0:1, SL:SL + 512], start=False, stop=True))
        if _P2NOO:
            return
        B.wait("tensor", f"p2_exp{i}_{n-1}")
        EPIORD = [6, 7, 0, 1, 2, 3, 4, 5]
        for g in range(i + 1):
            gg = grp_sched.index((i, g))
            vld = bf2 if gg % 2 == 0 else bf3
            B.wait("tensor", f"p2_vld{gg}")
            last_group = (g == i)
            if not last_group:
                for sc in range(4 * g, 4 * g + 4):
                    dcorder = [0, 1, 2, 3, 4, 5, 6, 7] if sc != 0 else [0, 1, 2, 3, 4, 5, "rb", 6, 7]
                    for dc in dcorder:
                        if dc == "rb":
                            emit_rbmm()
                            continue
                        if sc == 0:
                            if i >= 1:
                                B.wait("tensor", f"p2_obank{i-1}_{dc}")
                            if dc == 6 and not _P2NOSUM:
                                B.wait("tensor", f"p2_sumev{i}")
                            if dc == 7 and not _P2NOSUM:
                                B.wait("tensor", f"p2_rbev{i}")
                        fn = lambda e, sc=sc, dc=dc, vld=vld, n=n: e.matmul(
                            out=pb[OPERM[dc]][:], lhsT=vld[:, (sc % 4) * 1024 + dc * 128:(sc % 4) * 1024 + dc * 128 + 128],
                            rhs=pt[:, sc * 512:(sc + 1) * 512],
                            start=(sc == 0), stop=(sc == n - 1))
                        names = []
                        if dc == NDC - 1:
                            names.append(f"p2_Oc{i}_{sc}")
                            if sc % 4 == 3:
                                names.append(f"p2_Og{gg}")
                        if names:
                            B.mark("tensor", "pe", 1, names, fn)
                        else:
                            B.op("tensor", fn)
            else:
                if g == 0:
                    emit_rbmm()
                for dc in EPIORD:
                    for sc in range(4 * g, 4 * g + 4):
                        if g == 0 and sc == 4 * g:
                            if i >= 1:
                                B.wait("tensor", f"p2_obank{i-1}_{dc}")
                            if dc == 6 and not _P2NOSUM:
                                B.wait("tensor", f"p2_sumev{i}")
                            if dc == 7 and not _P2NOSUM:
                                B.wait("tensor", f"p2_rbev{i}")
                        q0 = 128 * (sc - (n - 4)) if (sc >= n - 4 and sc > 0) else 0
                        fn = lambda e, sc=sc, dc=dc, vld=vld, n=n, q0=q0: e.matmul(
                            out=pb[OPERM[dc]][:, q0:512], lhsT=vld[:, (sc % 4) * 1024 + dc * 128:(sc % 4) * 1024 + dc * 128 + 128],
                            rhs=pt[:, sc * 512 + q0:(sc + 1) * 512],
                            start=(sc == 0), stop=(sc == n - 1))
                        names = []
                        if sc == n - 1:
                            names.append(f"p2_O{i}_{dc}")
                            if dc == EPIORD[-1]:
                                names.append(f"p2_Og{gg}")
                        if dc == EPIORD[-1]:
                            names.append(f"p2_Oc{i}_{sc}")
                        if names:
                            B.mark("tensor", "pe", 1, names, fn)
                        else:
                            B.op("tensor", fn)

    def p2_act(i):
        n = 4 * (i + 1)
        for sc in range(n):
            if sc >= n - 4 and not _P2NOMASK:
                B.wait("scalar", f"p2_mask{i}_{sc}")
            else:
                B.wait("scalar", f"p2_S{i}_{sc}")
            if i >= 1 and sc < 4 * i:
                B.wait("scalar", f"p2_Oc{i-1}_{sc}")
            q0 = 128 * (sc - (n - 4)) if sc >= n - 4 else 0
            B.mark("scalar", "act", 1, f"p2_exp{i}_{sc}",
                   lambda e, sc=sc, q0=q0, bank=pb[1 + sc % 2]: e.activation(
                       out=pt[:, sc * 512 + q0:(sc + 1) * 512], in_=bank[:, q0:512], func=AF.Exp, bias=cm16[:]))
        if _P2NOSUM:
            return
        B.wait("scalar", f"p2_sumdone{i}")
        B.mark("scalar", "act", 1, f"p2_sumev{i}",
               lambda e: e.activation(out=rows[0:1, RECIP:RECIP + 512], in_=pb[0][0:1, :], func=AF.Ln))
        B.op("scalar", lambda e: e.activation(out=rows[0:1, RECIP2:RECIP2 + 512],
                                              in_=rows[0:1, RECIP:RECIP + 512], func=AF.Exp, scale=-1.0))
        B.mark("scalar", "act", 1, f"p2_recipA{i}",
               lambda e: e.activation(out=rbf[0:1, SH:SH + 512], in_=rows[0:1, RECIP2:RECIP2 + 512], func=AF.Copy))
        B.wait("scalar", f"p2_rbmm{i}")
        if i >= 1:
            B.wait("scalar", f"p2_epidone{i-1}")
        B.mark("scalar", "act", 1, f"p2_rbev{i}",
               lambda e: e.activation(out=rb[:], in_=pb[3][:], func=AF.Copy))

    def p2_dve(i):
        n = 4 * (i + 1)
        B.wait("vector", "ld_masks")
        for k in range(4) if not _P2NOMASK else []:
            sc = n - 4 + k
            q0 = 128 * k
            B.wait("vector", f"p2_S{i}_{sc}")
            B.mark("vector", "dve", 1, f"p2_mask{i}_{sc}",
                   lambda e, k=k, q0=q0, bank=pb[1 + (n - 4 + k) % 2]: e.tensor_tensor(
                       out=bank[:, q0:q0 + 128], in0=bank[:, q0:q0 + 128],
                       in1=masks[:], op=ALU.add))
        if _P2NOSUM or _P2NOO:
            return
        B.wait("vector", f"p2_recipA{i}")
        B.mark("vector", "dve", 1, f"p2_sl{i}",
               lambda e: e.tensor_tensor(out=rbf[0:1, SL:SL + 512], in0=rows[0:1, RECIP2:RECIP2 + 512],
                                         in1=rbf[0:1, SH:SH + 512], op=ALU.subtract))
        if _P2NOEPI:
            return
        B.wait("vector", f"p2_x{i}")
        if not _P2NOSUM:
            B.wait("vector", f"p2_rbev{i}")
        EPIORD = [6, 7, 0, 1, 2, 3, 4, 5]

        def emit_mult(pos):
            dc = EPIORD[pos]
            B.wait("vector", f"p2_O{i}_{dc}")
            B.mark("vector", "dve", 1, f"p2_obank{i}_{dc}",
                  lambda e, dc=dc, pos=pos: e.tensor_tensor(
                      out=tmp32[:, (pos % 4) * 512:(pos % 4) * 512 + 512],
                      in0=pb[OPERM[dc]][:], in1=rb[:], op=ALU.mult))

        def emit_add(pos):
            dc = EPIORD[pos]
            if pos >= 2:
                B.wait("vector", f"p2_ybdma{i}_{EPIORD[pos-2]}")
            elif i >= 1:
                B.wait("vector", f"p2_ybdma{i-1}_{EPIORD[pos+6]}")
            B.mark("vector", "dve", 1, f"p2_epi{i}_{dc}",
                   lambda e, dc=dc, pos=pos, i=i: e.tensor_tensor(
                       out=ystage[:, (pos % 2) * 512:(pos % 2) * 512 + 512],
                       in0=tmp32[:, (pos % 4) * 512:(pos % 4) * 512 + 512],
                       in1=xbuf[i % 2][:, dc * 512:(dc + 1) * 512], op=ALU.add))

        for pos in range(4):
            emit_mult(pos)
        for pos in range(4, 8):
            emit_add(pos - 4)
            emit_mult(pos)
        for pos in range(4, 8):
            emit_add(pos)
        B.alias(f"p2_epidone{i}", f"p2_epi{i}_{EPIORD[-1]}")

    def p2_gp(i):
        if _P2NOO or _P2NOEPI:
            return
        for dc in [6, 7, 0, 1, 2, 3, 4, 5]:
            B.wait("gpsimd", f"p2_epi{i}_{dc}")
            B.mark("gpsimd", "st", 16, f"p2_ydma{i}_{dc}",
                   lambda e, i=i, dc=dc: e.dma_start(out=yscr.ap()[:, i, dc],
                                                     in_=ystage[:, (dc % 2) * 512:(dc % 2) * 512 + 512]))
            B.mark("gpsimd", "st", 16, f"p2_ybdma{i}_{dc}",
                   lambda e, i=i, dc=dc: e.dma_start(out=ybscr.ap()[:, i, dc],
                                                     in_=ystage[:, (dc % 2) * 512:(dc % 2) * 512 + 512]))
        B.alias(f"p2_ybdone{i}", f"p2_ybdma{i}_5")

    if _PHASES < 2:
        B.waitval("sync", "st", B.cnt["st"])
        return _finish(nc, B)
    B.waitval("tensor", "act", p1_act_end)
    B.waitval("tensor", "dve", p1_dve_end)
    B.waitval("scalar", "pe", p1_pe_end)
    for i in range(_P2N):
        p2_sync(i)
    for i in range(_P2N):
        p2_pe(i)
    for i in range(_P2N):
        p2_act(i)
    for i in range(_P2N):
        p2_dve(i)
    for i in range(_P2N):
        p2_gp(i)

    p2_act_end = B.cnt["act"]
    p2_dve_end = B.cnt["dve"]
    p2_pe_end = B.cnt["pe"]
    p2_st_end = B.cnt["st"]

    ybufs = [ybuf2, xbuf[0], xbuf[1]]

    def ybuf(i):
        return ybufs[i % 3]

    def p3_dve_pre(i):
        B.wait("vector", f"p3_xb{i}")
        if i >= 1:
            B.wait("vector", f"p3_stats{i-1}")
        B.mark("vector", "dve", 1, f"p3_sq{i}",
               lambda e: e.tensor_tensor(out=sq_ap(), in0=xb_ap(), in1=xb_ap(), op=ALU.mult))

    def p3_act_mrsb(i):
        B.wait("scalar", f"p3_bcast{i}")
        B.mark("scalar", "act", 1, f"p3_mrsb{i}",
               lambda e: e.activation(out=mrsb[:], in_=pb[3][:], func=AF.Copy))

    def p3_gp_yn1(i):
        if i >= 2:
            j = i - 2
            for dc in range(NDC):
                B.wait("gpsimd", f"p3_outadd{j}_{dc}")
                B.mark("gpsimd", "out", 16, f"p3_outdma{j}_{dc}",
                       lambda e, j=j, dc=dc: e.dma_start(out=out_h.ap()[j][:, dc],
                                                         in_=ystage[:, (dc % 2) * 512:(dc % 2) * 512 + 512]))
        B.wait("gpsimd", f"p3_mrsb{i}")
        B.wait("gpsimd", f"p3_y{i}")
        for dc in range(NDC):
            if dc >= 2:
                B.wait("gpsimd", f"p3_ynb{i}_{dc-2}")
            elif i >= 1:
                B.wait("gpsimd", f"p3_ynb{i-1}_{dc+6}")
            B.mark("gpsimd", "pool", 1, f"p3_yna{i}_{dc}",
                   lambda e, i=i, dc=dc: e.tensor_tensor(
                       out=ynt[:, (dc % 2) * 512:(dc % 2) * 512 + 512],
                       in0=ybuf(i)[:, dc * 512:(dc + 1) * 512], in1=mrsb[:], op=ALU.subtract))

    def p3_dve_yn2(i):
        if i >= 2:
            B.wait("vector", f"p3_P{i-2}_1")
        for dc in range(NDC):
            B.wait("vector", f"p3_yna{i}_{dc}")
            fn = lambda e, dc=dc, i=i: e.tensor_tensor(
                out=yn_ap(i)[:, dc * 512:(dc + 1) * 512],
                in0=ynt[:, (dc % 2) * 512:(dc % 2) * 512 + 512], in1=pb[2][:], op=ALU.mult)
            names = [f"p3_ynb{i}_{dc}"]
            if dc == NDC - 1:
                names.append(f"p3_yn{i}")
            B.mark("vector", "dve", 1, names, fn)

    def p3_pe_ffn(i):
        B.wait("tensor", f"p3_yn{i}")
        B.wait("tensor", "ffn_v2")
        if i >= 1:
            B.wait("tensor", f"p3_outadd{i-1}_5")
        for rc in range(2):
            for dc in range(NDC):
                fn = lambda e, rc=rc, dc=dc, i=i: e.matmul(
                    out=pb[4 + rc][:], lhsT=wv[:, dc * 256 + rc * 128: dc * 256 + rc * 128 + 128],
                    rhs=yn_ap(i)[:, dc * 512:(dc + 1) * 512],
                    start=(dc == 0), stop=(dc == NDC - 1))
                if dc == NDC - 1:
                    B.mark("tensor", "pe", 1, f"p3_P{i}_{rc}", fn)
                else:
                    B.op("tensor", fn)

        def q2_group(g):
            hfm = bf4 if g % 2 == 0 else bf5
            for fc in range(8 * g, 8 * g + 8):
                B.wait("tensor", f"p3_hev{i}_{fc}")
                for rc in range(2):
                    fn = lambda e, fc=fc, rc=rc, hfm=hfm: e.matmul(
                        out=pb[4 + rc][:], lhsT=wq[:, fc * 256 + rc * 128: fc * 256 + rc * 128 + 128],
                        rhs=hfm[:, (fc % 8) * 512:(fc % 8) * 512 + 512],
                        start=(fc == 0), stop=(fc == 31))
                    if fc % 8 == 7 and rc == 1:
                        names = [f"p3_q2g{i}_{g}"]
                        if g == 3:
                            names.append(f"p3_q2{i}")
                        B.mark("tensor", "pe", 1, names, fn)
                    else:
                        B.op("tensor", fn)

        B.wait("tensor", f"p3_pev{i}")
        HB = [6, 7, 0, 1]

        def emit_h(fc):
            if fc >= 4:
                B.wait("tensor", f"p3_hev{i}_{fc-4}")
            if fc % 4 == 2:
                if i + 2 < NST:
                    B.wait("tensor", f"p3_rowsA{i+2}")
                else:
                    B.wait("tensor", f"p3_rowsA{NST-1}")
            for rc in range(2):
                fn = lambda e, fc=fc, rc=rc: e.matmul(
                    out=pb[HB[fc % 4]][:], lhsT=wk[:, rc * 4096 + fc * 128: rc * 4096 + fc * 128 + 128],
                    rhs=p_sb[:, rc * 512:(rc + 1) * 512],
                    start=(rc == 0), stop=(rc == 1))
                if rc == 1:
                    B.mark("tensor", "pe", 1, f"p3_H{i}_{fc}", fn)
                else:
                    B.op("tensor", fn)

        def emit_q2(fc):
            g = fc // 8
            hfm = bf4 if g % 2 == 0 else bf5
            B.wait("tensor", f"p3_hev{i}_{fc}")
            for rc in range(2):
                fn = lambda e, fc=fc, rc=rc, hfm=hfm: e.matmul(
                    out=pb[4 + rc][:], lhsT=wq[:, fc * 256 + rc * 128: fc * 256 + rc * 128 + 128],
                    rhs=hfm[:, (fc % 8) * 512:(fc % 8) * 512 + 512],
                    start=(fc == 0), stop=(fc == 31))
                if fc % 8 == 7 and rc == 1:
                    names = [f"p3_q2g{i}_{fc // 8}"]
                    if fc == 31:
                        names.append(f"p3_q2{i}")
                    B.mark("tensor", "pe", 1, names, fn)
                else:
                    B.op("tensor", fn)

        for fc in range(12):
            emit_h(fc)
        for fc in range(12, 32):
            emit_h(fc)
            emit_q2(fc - 12)
        for fc in range(20, 32):
            emit_q2(fc)
        B.wait("tensor", f"p3_q2ev{i}")
        for dc in range(NDC):
            if dc >= 4:
                B.wait("tensor", f"p3_outadd{i}_{dc-4}")
            for rc in range(2):
                fn = lambda e, dc=dc, rc=rc: e.matmul(
                    out=pb[4 + dc % 4][:],
                    lhsT=wv[:, 2048 + rc * 1024 + dc * 128: 2048 + rc * 1024 + dc * 128 + 128],
                    rhs=q2[:, rc * 512:(rc + 1) * 512],
                    start=(rc == 0), stop=(rc == 1))
                if rc == 1:
                    B.mark("tensor", "pe", 1, f"p3_o2{i}_{dc}", fn)
                else:
                    B.op("tensor", fn)

    def p3_act_ffn(i):
        B.wait("scalar", f"p3_P{i}_0")
        B.op("scalar", lambda e: e.activation(out=p_sb[:, 0:512], in_=pb[4][:], func=AF.Copy))
        B.wait("scalar", f"p3_P{i}_1")
        B.mark("scalar", "act", 1, f"p3_pev{i}",
               lambda e: e.activation(out=p_sb[:, 512:1024], in_=pb[5][:], func=AF.Copy))

        def hev(fc):
            g = fc // 8
            B.wait("scalar", f"p3_H{i}_{fc}")
            if g >= 2:
                B.wait("scalar", f"p3_q2g{i}_{g-2}")
            elif i >= 1:
                B.wait("scalar", f"p3_q2g{i-1}_{g+2}")
            B.mark("scalar", "act", 1, f"p3_hev{i}_{fc}",
                   lambda e, fc=fc, hfm=(bf4 if (fc // 8) % 2 == 0 else bf5): e.activation(
                       out=hfm[:, (fc % 8) * 512:(fc % 8) * 512 + 512], in_=pb[[6, 7, 0, 1][fc % 4]][:],
                       func=AF.Gelu_apprx_tanh, bias=b1s[:, fc:fc + 1]))

        j = i + 2
        if j < _P3N:
            ln_rows_actA("p3", j)
        for fc in range(0, 4):
            hev(fc)
        if j < _P3N:
            ln_rows_actB("p3", j)
        for fc in range(4, 8):
            hev(fc)
        if j < _P3N:
            ln_rows_actC("p3", j)
        if i + 1 < _P3N:
            p3_act_mrsb(i + 1)
        for fc in range(8, 32):
            hev(fc)
        B.wait("scalar", f"p3_q2{i}")
        B.op("scalar", lambda e: e.activation(out=q2[:, 0:512], in_=pb[4][:], func=AF.Copy))
        B.mark("scalar", "act", 1, f"p3_q2ev{i}",
               lambda e: e.activation(out=q2[:, 512:1024], in_=pb[5][:], func=AF.Copy))

    def p3_dve_out(i):
        for dc in range(NDC):
            B.wait("vector", f"p3_o2{i}_{dc}")
            if dc >= 2:
                B.wait("vector", f"p3_outdma{i}_{dc-2}")
            elif i >= 1:
                B.wait("vector", f"p3_outdma{i-1}_{dc+6}")
            B.mark("vector", "dve", 1, f"p3_outadd{i}_{dc}",
                   lambda e, i=i, dc=dc: e.scalar_tensor_tensor(
                       out=ystage[:, (dc % 2) * 512:(dc % 2) * 512 + 512],
                       in0=pb[4 + dc % 4][:], scalar=b2s[:, dc:dc + 1],
                       in1=ybuf(i)[:, dc * 512:(dc + 1) * 512],
                       op0=ALU.add, op1=ALU.add))

    def p3_sync(i):
        B.wait("sync", f"p2_ybdone{i}")
        if i >= 1:
            B.wait("sync", f"p3_stats{i-1}")
        B.mark("sync", "ld", 16, f"p3_xb{i}",
               lambda e, i=i: e.dma_start(out=pt[:, 0:4096].rearrange("p (c t) -> p c t", c=8),
                                          in_=ybscr.ap()[:, i]))
        B.wait("sync", f"p2_ydma{i}_5")
        if i == 1:
            B.wait("sync", f"p2_epidone{NST-2}")
        elif i == 2:
            B.wait("sync", f"p2_epidone{NST-1}")
        elif i >= 3:
            B.wait("sync", f"p3_outadd{i-3}_7")
        B.mark("sync", "ld", 16, f"p3_y{i}",
               lambda e, i=i: e.dma_start(out=ybuf(i)[:].rearrange("p (c t) -> p c t", c=8),
                                          in_=yscr.ap()[:, i]))

    if _PHASES < 3 or _P2N < NST:
        B.waitval("sync", "st", B.cnt["st"])
        return _finish(nc, B)
    B.waitval("tensor", "act", p2_act_end)
    B.waitval("tensor", "dve", p2_dve_end)
    B.waitval("scalar", "pe", p2_pe_end)
    B.waitval("vector", "st", p2_st_end)
    B.waitval("gpsimd", "dve", p2_dve_end)

    for i in range(_P3N):
        p3_sync(i)

    ln_rows_act("p3", 0)
    p3_act_mrsb(0)
    if _P3N > 1:
        ln_rows_act("p3", 1)
    for i in range(_P3N):
        p3_act_ffn(i)

    p3_dve_pre(0)
    ln_rows_dve("p3", 0)
    if _P3N > 1:
        p3_dve_pre(1)
        ln_rows_dve("p3", 1)
    p3_dve_yn2(0)
    if _P3N > 2:
        p3_dve_pre(2)
    for i in range(_P3N):
        if i + 2 < _P3N:
            ln_rows_dve("p3", i + 2)
        if i + 3 < _P3N:
            p3_dve_pre(i + 3)
        if i + 1 < _P3N:
            p3_dve_yn2(i + 1)
        p3_dve_out(i)

    for i in range(_P3N):
        p3_gp_yn1(i)
    for j in range(max(0, _P3N - 2), _P3N):
        for dc in range(NDC):
            B.wait("gpsimd", f"p3_outadd{j}_{dc}")
            B.mark("gpsimd", "out", 16, f"p3_outdma{j}_{dc}",
                   lambda e, j=j, dc=dc: e.dma_start(out=out_h.ap()[j][:, dc],
                                                     in_=ystage[:, (dc % 2) * 512:(dc % 2) * 512 + 512]))
    B.waitval("gpsimd", "out", B.cnt["out"])

    pe_stats("p3", 0, xb_ap, sq_ap)
    pe_bcast("p3", 0, f"p2_epidone{NST-1}")
    if _P3N > 1:
        pe_stats("p3", 1, xb_ap, sq_ap)
    for i in range(_P3N):
        if i + 1 < _P3N:
            pe_bcast("p3", i + 1, f"p3_yn{i}")
        if i + 2 < _P3N:
            pe_stats("p3", i + 2, xb_ap, sq_ap)
        p3_pe_ffn(i)

    return _finish(nc, B)


def _finish(nc, B):
    final = B.finalize()
    with nc.Block() as blk:
        @blk.sync
        def _(e):
            for fn in final["sync"]:
                fn(e)

        @blk.gpsimd
        def _(e):
            for fn in final["gpsimd"]:
                fn(e)

        @blk.scalar
        def _(e):
            for fn in final["scalar"]:
                fn(e)

        @blk.vector
        def _(e):
            for fn in final["vector"]:
                fn(e)

        @blk.tensor
        def _(e):
            for fn in final["tensor"]:
                fn(e)

    return nc


_NC_CACHE = None


def _get_nc():
    global _NC_CACHE
    if _NC_CACHE is None:
        _NC_CACHE = build_nc()
    return _NC_CACHE


def kernel(L0_attn_x, L0_attn_Wq, L0_attn_Wk, L0_attn_Wv,
           L0_ffn_U1, L0_ffn_V1, L0_ffn_b1, L0_ffn_U2, L0_ffn_V2, L0_ffn_b2):
    x = np.asarray(L0_attn_x, dtype=np.float32)
    Bsz, T, D = x.shape
    assert (Bsz, T, D) == (8, 4096, 1024)

    def wchunk(w):
        w = np.asarray(w, dtype=np.float32)
        nch = w.shape[0] // 128
        return np.ascontiguousarray(w.reshape(nch, 128, w.shape[1]).transpose(1, 0, 2))

    shared = dict(
        wq_h=wchunk(L0_attn_Wq), wk_h=wchunk(L0_attn_Wk), wv_h=wchunk(L0_attn_Wv),
        u1_h=wchunk(L0_ffn_U1), v1_h=wchunk(L0_ffn_V1),
        u2_h=wchunk(L0_ffn_U2), v2_h=wchunk(L0_ffn_V2),
        b1_h=np.ascontiguousarray(np.asarray(L0_ffn_b1, dtype=np.float32).reshape(32, 128).T),
        b2_h=np.ascontiguousarray(np.asarray(L0_ffn_b2, dtype=np.float32).reshape(8, 128).T),
    )
    p_idx = np.arange(128)[:, None]
    j_idx = np.arange(128)[None, :]
    shared["mk_h"] = np.where(j_idx >= p_idx, np.float32(0.0), np.float32(NEG)).astype(BFNP)

    in_maps = []
    for b in range(Bsz):
        xt = np.ascontiguousarray(x[b].T.reshape(NDC, 128, NST, ST).transpose(2, 1, 0, 3))
        in_maps.append(dict(xin=xt, **shared))

    nc = _get_nc()
    res = run_bass_kernel_spmd(nc, in_maps, core_ids=list(range(8)))

    outs = []
    for b in range(Bsz):
        o = np.asarray(res.results[b]["out_h"], dtype=np.float32)
        outs.append(o.transpose(0, 3, 2, 1).reshape(T, D))
    full = np.stack(outs)
    if _DEBUG:
        kernel._debug_results = res
    return full


# version 19
# speedup vs baseline: 1.0415x; 1.0415x over previous
import numpy as np
import ml_dtypes
import concourse.bass as bass
import concourse.mybir as mybir
from concourse.bass_utils import run_bass_kernel_spmd

F32 = mybir.dt.float32
BF16 = mybir.dt.bfloat16
AF = mybir.ActivationFunctionType
ALU = mybir.AluOpType
BFNP = ml_dtypes.bfloat16

ST = 512
NST = 8
NDC = 8
CSH = 16.0
NEG = -1.0e30

_DEBUG = False
_PHASES = 3
_P2N = NST
_P2NOMASK = False
_P2NOSUM = False
_P2NOO = False
_P2NOEPI = False
_P3N = NST

ENGS = ("sync", "gpsimd", "scalar", "vector", "tensor")
SEMS = ("ld", "st", "out", "pe", "act", "dve", "pool")




class Bld:

    def __init__(self):
        self.ops = {e: [] for e in ENGS}
        self.sems = {}
        self.cnt = {s: 0 for s in SEMS}
        self.ev = {}

    def op(self, eng, fn):
        self.ops[eng].append(("call", fn))

    def mark(self, eng, sem, inc, names, fn):
        self.cnt[sem] += inc
        val = self.cnt[sem]
        if isinstance(names, str):
            names = [names]
        for n in names:
            assert n not in self.ev, n
            self.ev[n] = (sem, val, eng)
        self.ops[eng].append(
            ("call", lambda e, fn=fn, sem=sem, inc=inc: fn(e).then_inc(self.sems[sem], inc), sem, inc, names[0]))

    def alias(self, newname, name):
        self.ev[newname] = self.ev[name]

    def wait(self, eng, name):
        self.ops[eng].append(("wait", name))

    def waitval(self, eng, sem, val):
        self.ops[eng].append(("waitval", sem, val))

    def finalize(self):
        final = {}
        for eng in ENGS:
            out = []
            lastw = {}
            for item in self.ops[eng]:
                if item[0] == "call":
                    out.append(item[1])
                    continue
                if item[0] == "wait":
                    if item[1] not in self.ev:
                        if _PHASES < 3 or _P2N < NST or _P2NOMASK or _P2NOSUM or _P2NOO or _P2NOEPI or _P3N < NST:
                            continue
                        raise KeyError(item[1])
                    sem, val, prod = self.ev[item[1]]
                    if prod == eng:
                        continue
                else:
                    sem, val = item[1], item[2]
                if lastw.get(sem, 0) >= val:
                    continue
                lastw[sem] = val
                out.append(lambda e, sem=sem, val=val: e.wait_ge(self.sems[sem], val))
            final[eng] = out
        return final


def build_nc():
    nc = bass.Bass("TRN2", target_bir_lowering=False)

    xin = nc.dram_tensor("xin", [NST, 128, NDC, ST], F32, kind="ExternalInput")
    wq_h = nc.dram_tensor("wq_h", [128, 8, 1024], F32, kind="ExternalInput")
    wk_h = nc.dram_tensor("wk_h", [128, 8, 1024], F32, kind="ExternalInput")
    wv_h = nc.dram_tensor("wv_h", [128, 8, 1024], F32, kind="ExternalInput")
    u1_h = nc.dram_tensor("u1_h", [128, 8, 256], F32, kind="ExternalInput")
    v1_h = nc.dram_tensor("v1_h", [128, 2, 4096], F32, kind="ExternalInput")
    u2_h = nc.dram_tensor("u2_h", [128, 32, 256], F32, kind="ExternalInput")
    v2_h = nc.dram_tensor("v2_h", [128, 2, 1024], F32, kind="ExternalInput")
    b1_h = nc.dram_tensor("b1_h", [128, 32], F32, kind="ExternalInput")
    b2_h = nc.dram_tensor("b2_h", [128, 8], F32, kind="ExternalInput")
    mk_h = nc.dram_tensor("mk_h", [128, 128], BF16, kind="ExternalInput")
    out_h = nc.dram_tensor("out_h", [NST, 128, NDC, ST], F32, kind="ExternalOutput")

    skind = "ExternalOutput" if _DEBUG else "Internal"
    qscr = nc.dram_tensor("qscr", [128, NST, 8, ST], BF16, kind=skind)
    kscr = nc.dram_tensor("kscr", [128, 32, 8, 128], BF16, kind=skind)
    vscr = nc.dram_tensor("vscr", [128, 32, 1024], BF16, kind=skind)
    yscr = nc.dram_tensor("yscr", [128, NST, NDC, ST], F32, kind=skind)
    ybscr = nc.dram_tensor("ybscr", [128, NST, NDC, ST], BF16, kind=skind)

    wq = nc.alloc_sbuf_tensor("wq", [128, 8192], BF16)
    wk = nc.alloc_sbuf_tensor("wk", [128, 8192], BF16)
    wv = nc.alloc_sbuf_tensor("wv", [128, 8192], BF16)
    pt = nc.alloc_sbuf_tensor("pt", [128, 16384], BF16)
    qsm = nc.alloc_sbuf_tensor("qsm", [128, 1024], BF16)
    bf2 = nc.alloc_sbuf_tensor("bf2", [128, 4096], BF16)
    bf3 = nc.alloc_sbuf_tensor("bf3", [128, 4096], BF16)
    bf4 = nc.alloc_sbuf_tensor("bf4", [128, 4096], BF16)
    bf5 = nc.alloc_sbuf_tensor("bf5", [128, 4096], BF16)
    xbuf = [nc.alloc_sbuf_tensor("xbuf0", [128, 4096], F32),
            nc.alloc_sbuf_tensor("xbuf1", [128, 4096], F32)]
    masks = nc.alloc_sbuf_tensor("masks", [128, 128], BF16)
    rows = nc.alloc_sbuf_tensor("rows", [128, 2560], F32)
    rbf = nc.alloc_sbuf_tensor("rbf", [128, 2048], BF16)
    tmp32 = nc.alloc_sbuf_tensor("tmp32", [128, 2048], F32)
    ynt = nc.alloc_sbuf_tensor("ynt", [128, 1024], F32)
    rb = nc.alloc_sbuf_tensor("rb", [128, 512], F32)
    mrsb = nc.alloc_sbuf_tensor("mrsb", [128, 512], F32)
    ystage = nc.alloc_sbuf_tensor("ystage", [128, 1024], F32)
    qld = nc.alloc_sbuf_tensor("qld", [128, 4096], BF16)
    ybuf2 = nc.alloc_sbuf_tensor("ybuf2", [128, 4096], F32)
    p_sb = nc.alloc_sbuf_tensor("p_sb", [128, 1024], BF16)
    q2 = qsm
    b1s = nc.alloc_sbuf_tensor("b1s", [128, 32], F32)
    b2s = nc.alloc_sbuf_tensor("b2s", [128, 8], F32)
    onesrow = nc.alloc_sbuf_tensor("onesrow", [1, 128], BF16)
    statcol = nc.alloc_sbuf_tensor("statcol", [128, 2], BF16)
    cm16 = nc.alloc_sbuf_tensor("cm16", [128, 1], F32)
    ceps = nc.alloc_sbuf_tensor("ceps", [128, 1], F32)

    pb = [nc.alloc_psum_tensor(f"pb{k}", [128, 512], F32) for k in range(8)]

    B = Bld()
    for s in SEMS:
        B.sems[s] = nc.alloc_semaphore("sem_" + s)

    MEAN, MSQ, LNV, VAR, MR, RSTD = 0, 512, 1024, 1536, 2048, 512
    RECIP, RECIP2 = 512, 1024
    RH, RL, MH, ML = 0, 512, 1024, 1536
    SH, SL = 0, 512

    B.mark("sync", "ld", 16, "ld_masks",
           lambda e: e.dma_start(out=masks[:], in_=mk_h.ap()))
    B.mark("sync", "ld", 16, "ld_b1", lambda e: e.dma_start(out=b1s[:], in_=b1_h.ap()))
    B.mark("sync", "ld", 16, "ld_b2", lambda e: e.dma_start(out=b2s[:], in_=b2_h.ap()))

    B.mark("gpsimd", "pool", 1, "cst_ones", lambda e: e.memset(onesrow[:], 1.0))
    B.mark("gpsimd", "pool", 1, "cst_s0", lambda e: e.memset(statcol[:, 0:1], 1.0 / 1024.0))
    B.mark("gpsimd", "pool", 1, "cst_s1", lambda e: e.memset(statcol[:, 1:2], 1.0))
    B.mark("gpsimd", "pool", 1, "cst_m16", lambda e: e.memset(cm16[:], -CSH))
    B.mark("gpsimd", "pool", 1, "cst_done", lambda e: e.memset(ceps[:], 1e-5))
    B.waitval("gpsimd", "ld", 64)
    B.mark("gpsimd", "st", 16, "w_q",
           lambda e: e.dma_start(out=wq[:].rearrange("p (c n) -> p c n", c=8), in_=wq_h.ap()))
    B.mark("gpsimd", "st", 16, "w_k",
           lambda e: e.dma_start(out=wk[:].rearrange("p (c n) -> p c n", c=8), in_=wk_h.ap()))
    B.mark("gpsimd", "st", 16, "w_v",
           lambda e: e.dma_start(out=wv[:].rearrange("p (c n) -> p c n", c=8), in_=wv_h.ap()))

    def xb_ap():
        return pt[:, 0:4096]

    def sq_ap():
        return pt[:, 4096:8192]

    def xn_ap(i):
        return pt[:, 8192 + (i % 2) * 4096: 12288 + (i % 2) * 4096]

    def yn_ap(i):
        return pt[:, 8192 + (i % 2) * 4096: 12288 + (i % 2) * 4096]

    def ln_rows_actA(ph, i):
        B.wait("scalar", f"{ph}_stats{i}")
        B.op("scalar", lambda e: e.activation(out=rows[0:1, MEAN:MEAN + 512], in_=pb[0][0:1, :], func=AF.Copy))
        B.op("scalar", lambda e: e.activation(out=rows[0:1, MSQ:MSQ + 512], in_=pb[1][0:1, :], func=AF.Copy))
        B.mark("scalar", "act", 1, f"{ph}_rowsA{i}",
               lambda e: e.activation(out=rows[0:1, LNV:LNV + 512], in_=rows[0:1, MEAN:MEAN + 512], func=AF.Square))

    def ln_rows_actB(ph, i):
        B.wait("scalar", f"{ph}_var{i}")
        B.op("scalar", lambda e: e.activation(out=rows[0:1, LNV:LNV + 512], in_=rows[0:1, VAR:VAR + 512],
                                              func=AF.Ln, bias=ceps[0:1, :]))
        B.op("scalar", lambda e: e.activation(out=rows[0:1, RSTD:RSTD + 512], in_=rows[0:1, LNV:LNV + 512],
                                              func=AF.Exp, scale=-0.5))
        B.mark("scalar", "act", 1, f"{ph}_rowsB{i}",
               lambda e: e.activation(out=rbf[0:1, RH:RH + 512], in_=rows[0:1, RSTD:RSTD + 512], func=AF.Copy))

    def ln_rows_actC(ph, i):
        B.wait("scalar", f"{ph}_mr{i}")
        B.mark("scalar", "act", 1, f"{ph}_rowsC{i}",
               lambda e: e.activation(out=rbf[0:1, MH:MH + 512], in_=rows[0:1, MR:MR + 512], func=AF.Copy))

    def ln_rows_act(ph, i):
        B.wait("scalar", f"{ph}_stats{i}")
        B.op("scalar", lambda e: e.activation(out=rows[0:1, MEAN:MEAN + 512], in_=pb[0][0:1, :], func=AF.Copy))
        B.op("scalar", lambda e: e.activation(out=rows[0:1, MSQ:MSQ + 512], in_=pb[1][0:1, :], func=AF.Copy))
        B.mark("scalar", "act", 1, f"{ph}_rowsA{i}",
               lambda e: e.activation(out=rows[0:1, LNV:LNV + 512], in_=rows[0:1, MEAN:MEAN + 512], func=AF.Square))
        B.wait("scalar", f"{ph}_var{i}")
        B.op("scalar", lambda e: e.activation(out=rows[0:1, LNV:LNV + 512], in_=rows[0:1, VAR:VAR + 512],
                                              func=AF.Ln, bias=ceps[0:1, :]))
        B.op("scalar", lambda e: e.activation(out=rows[0:1, RSTD:RSTD + 512], in_=rows[0:1, LNV:LNV + 512],
                                              func=AF.Exp, scale=-0.5))
        B.mark("scalar", "act", 1, f"{ph}_rowsB{i}",
               lambda e: e.activation(out=rbf[0:1, RH:RH + 512], in_=rows[0:1, RSTD:RSTD + 512], func=AF.Copy))
        B.wait("scalar", f"{ph}_mr{i}")
        B.mark("scalar", "act", 1, f"{ph}_rowsC{i}",
               lambda e: e.activation(out=rbf[0:1, MH:MH + 512], in_=rows[0:1, MR:MR + 512], func=AF.Copy))

    def ln_rows_dve_var(ph, i):
        B.wait("vector", f"{ph}_rowsA{i}")
        B.mark("vector", "dve", 1, f"{ph}_var{i}",
               lambda e: e.tensor_tensor(out=rows[0:1, VAR:VAR + 512], in0=rows[0:1, MSQ:MSQ + 512],
                                         in1=rows[0:1, LNV:LNV + 512], op=ALU.subtract))

    def ln_rows_dve_mr(ph, i):
        B.wait("vector", f"{ph}_rowsB{i}")
        B.op("vector", lambda e: e.tensor_tensor(out=rows[0:1, MR:MR + 512], in0=rows[0:1, MEAN:MEAN + 512],
                                                 in1=rows[0:1, RSTD:RSTD + 512], op=ALU.mult))
        B.mark("vector", "dve", 1, f"{ph}_mr{i}",
               lambda e: e.tensor_tensor(out=rbf[0:1, RL:RL + 512], in0=rows[0:1, RSTD:RSTD + 512],
                                         in1=rbf[0:1, RH:RH + 512], op=ALU.subtract))

    def ln_rows_dve_ml(ph, i):
        B.wait("vector", f"{ph}_rowsC{i}")
        B.mark("vector", "dve", 1, f"{ph}_ml{i}",
               lambda e: e.tensor_tensor(out=rbf[0:1, ML:ML + 512], in0=rows[0:1, MR:MR + 512],
                                         in1=rbf[0:1, MH:MH + 512], op=ALU.subtract))

    def ln_rows_dve(ph, i):
        ln_rows_dve_var(ph, i)
        ln_rows_dve_mr(ph, i)
        ln_rows_dve_ml(ph, i)

    def pe_stats(ph, i, src_xb, src_sq):
        B.wait("tensor", f"{ph}_xb{i}")
        B.wait("tensor", f"{ph}_sq{i}")
        if i >= 1:
            B.wait("tensor", f"{ph}_rowsA{i-1}")
        B.wait("tensor", "cst_done")
        for c in range(NDC):
            B.op("tensor", lambda e, c=c: e.matmul(out=pb[0][0:1, :], lhsT=statcol[:, 0:1],
                                                   rhs=src_xb()[:, c * 512:(c + 1) * 512],
                                                   start=(c == 0), stop=(c == NDC - 1)))
        for c in range(NDC):
            fn = lambda e, c=c: e.matmul(out=pb[1][0:1, :], lhsT=statcol[:, 0:1],
                                         rhs=src_sq()[:, c * 512:(c + 1) * 512],
                                         start=(c == 0), stop=(c == NDC - 1))
            if c == NDC - 1:
                B.mark("tensor", "pe", 1, f"{ph}_stats{i}", fn)
            else:
                B.op("tensor", fn)

    def pe_bcast(ph, i, consumer_ev):
        B.wait("tensor", f"{ph}_ml{i}")
        if consumer_ev is not None:
            B.wait("tensor", consumer_ev)
        B.op("tensor", lambda e: e.matmul(out=pb[2][:], lhsT=onesrow[0:1, :], rhs=rbf[0:1, RH:RH + 512], start=True, stop=False))
        B.op("tensor", lambda e: e.matmul(out=pb[2][:], lhsT=onesrow[0:1, :], rhs=rbf[0:1, RL:RL + 512], start=False, stop=True))
        B.op("tensor", lambda e: e.matmul(out=pb[3][:], lhsT=onesrow[0:1, :], rhs=rbf[0:1, MH:MH + 512], start=True, stop=False))
        B.mark("tensor", "pe", 1, f"{ph}_bcast{i}",
               lambda e: e.matmul(out=pb[3][:], lhsT=onesrow[0:1, :], rhs=rbf[0:1, ML:ML + 512], start=False, stop=True))

    qkv_banks = []

    def p1_dve_pre(i):
        B.wait("vector", f"p1_x{i}")
        if i >= 1:
            B.wait("vector", f"p1_stats{i-1}")
        B.mark("vector", "dve", 1, f"p1_xb{i}",
               lambda e, i=i: e.tensor_copy(out=xb_ap(), in_=xbuf[i % 2][:]))
        B.mark("vector", "dve", 1, f"p1_sq{i}",
               lambda e: e.tensor_tensor(out=sq_ap(), in0=xb_ap(), in1=xb_ap(), op=ALU.mult))

    def p1_dve_xn(i):
        B.wait("vector", f"p1_bcast{i}")
        if i >= 2:
            B.wait("vector", f"p1_qkv{i-2}")
        for c in range(NDC):
            B.op("vector", lambda e, c=c, i=i: e.tensor_tensor(
                out=tmp32[:, (c % 2) * 512:(c % 2) * 512 + 512],
                in0=xbuf[i % 2][:, c * 512:(c + 1) * 512], in1=pb[2][:], op=ALU.mult))
            fn = lambda e, c=c, i=i: e.tensor_tensor(
                out=xn_ap(i)[:, c * 512:(c + 1) * 512],
                in0=tmp32[:, (c % 2) * 512:(c % 2) * 512 + 512], in1=pb[3][:], op=ALU.subtract)
            if c == NDC - 1:
                B.mark("vector", "dve", 1, f"p1_xn{i}", fn)
            else:
                B.op("vector", fn)

    def p1_pe_qkv(i):
        B.wait("tensor", f"p1_xn{i}")
        for kind, w in (("q", wq), ("k", wk)):
            B.wait("tensor", "w_" + kind)
            for ec in range(NDC):
                if len(qkv_banks) >= 4:
                    B.wait("tensor", qkv_banks[-4])
                for dc in range(NDC):
                    fn = lambda e, dc=dc, ec=ec, i=i, w=w, bank=pb[4 + len(qkv_banks) % 4]: e.matmul(
                        out=bank[:], lhsT=w[:, dc * 1024 + ec * 128: dc * 1024 + ec * 128 + 128],
                        rhs=xn_ap(i)[:, dc * 512:(dc + 1) * 512],
                        start=(dc == 0), stop=(dc == NDC - 1))
                    if dc == NDC - 1:
                        B.mark("tensor", "pe", 1, f"p1_mm{kind}{i}_{ec}", fn)
                    else:
                        B.op("tensor", fn)
                qkv_banks.append(f"p1_ev{kind}{i}_{ec}")
        B.wait("tensor", "w_v")
        for k in range(4):
            for h in range(2):
                if len(qkv_banks) >= 4:
                    B.wait("tensor", qkv_banks[-4])
                for dc in range(NDC):
                    fn = lambda e, dc=dc, k=k, h=h, i=i, bank=pb[4 + len(qkv_banks) % 4]: e.matmul(
                        out=bank[:], lhsT=xn_ap(i)[:, dc * 512 + k * 128: dc * 512 + k * 128 + 128],
                        rhs=wv[:, dc * 1024 + h * 512: dc * 1024 + h * 512 + 512],
                        start=(dc == 0), stop=(dc == NDC - 1))
                    if dc == NDC - 1:
                        B.mark("tensor", "pe", 1, f"p1_mmv{i}_{k}_{h}", fn)
                    else:
                        B.op("tensor", fn)
                qkv_banks.append(f"p1_evv{i}_{k}_{h}")
        B.alias(f"p1_qkv{i}", f"p1_mmv{i}_3_1")

    def p1_act_evacs(i):
        for ec in range(NDC):
            B.wait("scalar", f"p1_mmq{i}_{ec}")
            if ec >= 2:
                B.wait("scalar", f"p1_qdma{i}_{ec-2}")
            elif i >= 1:
                B.wait("scalar", f"p1_qdma{i-1}_{ec+6}")
            B.mark("scalar", "act", 1, f"p1_evq{i}_{ec}",
                   lambda e, ec=ec, bank=pb[4 + ec % 4]: e.activation(
                       out=qsm[:, (ec % 2) * 512:(ec % 2) * 512 + 512], in_=bank[:], func=AF.Copy, scale=1.0 / 32.0))
        if i >= 1:
            B.wait("scalar", f"p1_kdma{i-1}")
        kv = bf2[:].rearrange("p (s e t) -> p s e t", s=4, e=8)
        for ec in range(NDC):
            B.wait("scalar", f"p1_mmk{i}_{ec}")
            B.mark("scalar", "act", 1, f"p1_evk{i}_{ec}",
                   lambda e, ec=ec, kv=kv, bank=pb[4 + (8 + ec) % 4]: e.activation(
                       out=kv[:, :, ec, :], in_=bank[:], func=AF.Copy))
        if i >= 1:
            B.wait("scalar", f"p1_vdma{i-1}")
        for k in range(4):
            for h in range(2):
                B.wait("scalar", f"p1_mmv{i}_{k}_{h}")
                B.mark("scalar", "act", 1, f"p1_evv{i}_{k}_{h}",
                       lambda e, k=k, h=h, bank=pb[4 + (16 + k * 2 + h) % 4]: e.activation(
                           out=bf3[:, k * 1024 + h * 512: k * 1024 + h * 512 + 512], in_=bank[:], func=AF.Copy))

    def p1_gp_dmas(i):
        for ec in range(NDC):
            B.wait("gpsimd", f"p1_evq{i}_{ec}")
            B.mark("gpsimd", "st", 16, f"p1_qdma{i}_{ec}",
                   lambda e, i=i, ec=ec: e.dma_start(out=qscr.ap()[:, i, ec],
                                                     in_=qsm[:, (ec % 2) * 512:(ec % 2) * 512 + 512]))
        B.wait("gpsimd", f"p1_evk{i}_{NDC-1}")
        B.mark("gpsimd", "st", 16, f"p1_kdma{i}",
               lambda e, i=i: e.dma_start(out=kscr.ap()[:, 4 * i:4 * i + 4],
                                          in_=bf2[:].rearrange("p (s e t) -> p s e t", s=4, e=8)))
        B.wait("gpsimd", f"p1_evv{i}_3_1")
        B.mark("gpsimd", "st", 16, f"p1_vdma{i}",
               lambda e, i=i: e.dma_start(out=vscr.ap()[:, 4 * i:4 * i + 4],
                                          in_=bf3[:].rearrange("p (s d) -> p s d", s=4)))

    for i in range(NST):
        if i >= 2:
            B.wait("sync", f"p1_xn{i-2}")
        elif i == 1:
            B.waitval("sync", "ld", 64)
        B.mark("sync", "ld", 16, f"p1_x{i}",
               lambda e, i=i: e.dma_start(out=xbuf[i % 2][:].rearrange("p (c t) -> p c t", c=8),
                                          in_=xin.ap()[i]))

    ln_rows_act("p1", 0)
    ln_rows_act("p1", 1)
    for i in range(NST):
        if i + 2 < NST:
            ln_rows_act("p1", i + 2)
        p1_act_evacs(i)

    p1_dve_pre(0)
    ln_rows_dve("p1", 0)
    p1_dve_pre(1)
    ln_rows_dve("p1", 1)
    p1_dve_xn(0)
    p1_dve_pre(2)
    for i in range(NST):
        if i + 2 < NST:
            ln_rows_dve("p1", i + 2)
        if i + 1 < NST:
            p1_dve_xn(i + 1)
        if i + 3 < NST:
            p1_dve_pre(i + 3)

    for i in range(NST):
        p1_gp_dmas(i)

    pe_stats("p1", 0, xb_ap, sq_ap)
    pe_bcast("p1", 0, None)
    pe_stats("p1", 1, xb_ap, sq_ap)
    for i in range(NST):
        if i + 1 < NST:
            pe_bcast("p1", i + 1, f"p1_xn{i}" if i >= 0 else None)
        if i + 2 < NST:
            pe_stats("p1", i + 2, xb_ap, sq_ap)
        p1_pe_qkv(i)

    p1_act_end = B.cnt["act"]
    p1_dve_end = B.cnt["dve"]
    p1_pe_end = B.cnt["pe"]

    B.wait("gpsimd", f"p1_qkv{NST-1}")
    B.mark("gpsimd", "st", 16, "ffn_u2",
           lambda e: e.dma_start(out=wq[:].rearrange("p (c n) -> p c n", c=32), in_=u2_h.ap()))
    B.mark("gpsimd", "st", 16, "ffn_v1",
           lambda e: e.dma_start(out=wk[:].rearrange("p (c n) -> p c n", c=2), in_=v1_h.ap()))
    B.mark("gpsimd", "st", 16, "ffn_u1",
           lambda e: e.dma_start(out=wv[:, 0:2048].rearrange("p (c n) -> p c n", c=8), in_=u1_h.ap()))
    B.mark("gpsimd", "st", 16, "ffn_v2",
           lambda e: e.dma_start(out=wv[:, 2048:4096].rearrange("p (c n) -> p c n", c=2), in_=v2_h.ap()))

    grp_sched = []
    for i in range(NST):
        for g in range(i + 1):
            grp_sched.append((i, g))

    def p2_sync(i):
        B.wait("sync", f"p1_qdma{i}_{NDC-1}")
        if i >= 1:
            B.wait("sync", f"p2_S{i-1}_{4*i-1}")
        B.mark("sync", "ld", 16, f"p2_q{i}",
               lambda e, i=i: e.dma_start(out=qld[:].rearrange("p (c t) -> p c t", c=8), in_=qscr.ap()[:, i]))
        for g in range(i + 1):
            gg = grp_sched.index((i, g))
            B.wait("sync", f"p1_kdma{g}")
            if gg >= 2:
                B.wait("sync", f"p2_Sg{gg-2}")
            B.mark("sync", "ld", 16, f"p2_kld{gg}",
                   lambda e, g=g, kld=(bf4 if gg % 2 == 0 else bf5): e.dma_start(
                       out=kld[:].rearrange("p (s e t) -> p s e t", s=4, e=8),
                       in_=kscr.ap()[:, 4 * g:4 * g + 4]))
        if not _P2NOO:
            if i >= 2:
                B.wait("sync", f"p2_epidone{i-2}")
            else:
                B.wait("sync", f"p1_xn{NST-2+i}")
            B.mark("sync", "ld", 16, f"p2_x{i}",
               lambda e, i=i: e.dma_start(out=xbuf[i % 2][:].rearrange("p (c t) -> p c t", c=8), in_=xin.ap()[i]))
        for g in range(i + 1):
            gg = grp_sched.index((i, g))
            if _P2NOO:
                continue
            B.wait("sync", f"p1_vdma{g}")
            B.wait("sync", f"p1_kdma{NST-1}" if gg % 2 == 0 else f"p1_vdma{NST-1}")
            if gg >= 2:
                B.wait("sync", f"p2_Og{gg-2}")
            B.mark("sync", "ld", 16, f"p2_vld{gg}",
                   lambda e, g=g, vld=(bf2 if gg % 2 == 0 else bf3): e.dma_start(
                       out=vld[:].rearrange("p (s d) -> p s d", s=4),
                       in_=vscr.ap()[:, 4 * g:4 * g + 4]))

    OPERM = [1, 2, 4, 5, 6, 7, 0, 3]

    def p2_pe(i):
        n = 4 * (i + 1)
        B.wait("tensor", f"p2_q{i}")
        for sc in range(n):
            gg = grp_sched.index((i, sc // 4))
            kld = bf4 if gg % 2 == 0 else bf5
            B.wait("tensor", f"p2_kld{gg}")
            if sc >= 2:
                B.wait("tensor", f"p2_exp{i}_{sc-2}")
            elif i >= 1:
                B.wait("tensor", f"p2_obank{i-1}_{sc}")
            bank = pb[1 + sc % 2]
            q0 = 128 * (sc - (n - 4)) if sc >= n - 4 else 0
            for ec in range(NDC):
                fn = lambda e, sc=sc, ec=ec, kld=kld, bank=bank, q0=q0: e.matmul(
                    out=bank[:, q0:512], lhsT=kld[:, (sc % 4) * 1024 + ec * 128:(sc % 4) * 1024 + ec * 128 + 128],
                    rhs=qld[:, ec * 512 + q0:(ec + 1) * 512],
                    start=(ec == 0), stop=(ec == NDC - 1))
                if ec == NDC - 1:
                    names = [f"p2_S{i}_{sc}"]
                    if sc % 4 == 3:
                        names.append(f"p2_Sg{gg}")
                    B.mark("tensor", "pe", 1, names, fn)
                else:
                    B.op("tensor", fn)
            if sc >= 1 and not _P2NOSUM:
                scp = sc - 1
                qp = 128 * (scp - (n - 4)) if scp >= n - 4 else 0
                B.wait("tensor", f"p2_exp{i}_{scp}")
                if scp == 0 and i >= 1:
                    B.wait("tensor", f"p2_obank{i-1}_6")
                B.op("tensor", lambda e, scp=scp, n=n, qp=qp: e.matmul(
                    out=pb[0][0:1, qp:512], lhsT=statcol[:, 1:2], rhs=pt[:, scp * 512 + qp:(scp + 1) * 512],
                    start=(scp == 0), stop=(scp == n - 1)))
        if not _P2NOSUM:
            B.wait("tensor", f"p2_exp{i}_{n-1}")
            B.mark("tensor", "pe", 1, f"p2_sumdone{i}",
                   lambda e, n=n: e.matmul(out=pb[0][0:1, 384:512], lhsT=statcol[:, 1:2],
                                           rhs=pt[:, (n - 1) * 512 + 384: n * 512], start=False, stop=True))

        def emit_rbmm():
            if _P2NOSUM:
                return
            B.wait("tensor", f"p2_recipA{i}")
            B.wait("tensor", f"p2_sl{i}")
            if i >= 1:
                B.wait("tensor", f"p2_obank{i-1}_7")
            B.op("tensor", lambda e: e.matmul(out=pb[3][:], lhsT=onesrow[0:1, :], rhs=rbf[0:1, SH:SH + 512], start=True, stop=False))
            B.mark("tensor", "pe", 1, f"p2_rbmm{i}",
                   lambda e: e.matmul(out=pb[3][:], lhsT=onesrow[0:1, :], rhs=rbf[0:1, SL:SL + 512], start=False, stop=True))
        if _P2NOO:
            return
        B.wait("tensor", f"p2_exp{i}_{n-1}")
        EPIORD = [6, 7, 0, 1, 2, 3, 4, 5]
        for g in range(i + 1):
            gg = grp_sched.index((i, g))
            vld = bf2 if gg % 2 == 0 else bf3
            B.wait("tensor", f"p2_vld{gg}")
            last_group = (g == i)
            if not last_group:
                for sc in range(4 * g, 4 * g + 4):
                    dcorder = [0, 1, 2, 3, 4, 5, 6, 7] if sc != 0 else [0, 1, 2, 3, 4, 5, "rb", 6, 7]
                    for dc in dcorder:
                        if dc == "rb":
                            emit_rbmm()
                            continue
                        if sc == 0:
                            if i >= 1:
                                B.wait("tensor", f"p2_obank{i-1}_{dc}")
                            if dc == 6 and not _P2NOSUM:
                                B.wait("tensor", f"p2_sumev{i}")
                            if dc == 7 and not _P2NOSUM:
                                B.wait("tensor", f"p2_rbev{i}")
                        fn = lambda e, sc=sc, dc=dc, vld=vld, n=n: e.matmul(
                            out=pb[OPERM[dc]][:], lhsT=vld[:, (sc % 4) * 1024 + dc * 128:(sc % 4) * 1024 + dc * 128 + 128],
                            rhs=pt[:, sc * 512:(sc + 1) * 512],
                            start=(sc == 0), stop=(sc == n - 1))
                        names = []
                        if dc == NDC - 1:
                            names.append(f"p2_Oc{i}_{sc}")
                            if sc % 4 == 3:
                                names.append(f"p2_Og{gg}")
                        if names:
                            B.mark("tensor", "pe", 1, names, fn)
                        else:
                            B.op("tensor", fn)
            else:
                if g == 0:
                    emit_rbmm()
                for dc in EPIORD:
                    for sc in range(4 * g, 4 * g + 4):
                        if g == 0 and sc == 4 * g:
                            if i >= 1:
                                B.wait("tensor", f"p2_obank{i-1}_{dc}")
                            if dc == 6 and not _P2NOSUM:
                                B.wait("tensor", f"p2_sumev{i}")
                            if dc == 7 and not _P2NOSUM:
                                B.wait("tensor", f"p2_rbev{i}")
                        q0 = 128 * (sc - (n - 4)) if (sc >= n - 4 and sc > 0) else 0
                        fn = lambda e, sc=sc, dc=dc, vld=vld, n=n, q0=q0: e.matmul(
                            out=pb[OPERM[dc]][:, q0:512], lhsT=vld[:, (sc % 4) * 1024 + dc * 128:(sc % 4) * 1024 + dc * 128 + 128],
                            rhs=pt[:, sc * 512 + q0:(sc + 1) * 512],
                            start=(sc == 0), stop=(sc == n - 1))
                        names = []
                        if sc == n - 1:
                            names.append(f"p2_O{i}_{dc}")
                            if dc == EPIORD[-1]:
                                names.append(f"p2_Og{gg}")
                        if dc == EPIORD[-1]:
                            names.append(f"p2_Oc{i}_{sc}")
                        if names:
                            B.mark("tensor", "pe", 1, names, fn)
                        else:
                            B.op("tensor", fn)

    def p2_act(i):
        n = 4 * (i + 1)
        for sc in range(n):
            if sc >= n - 4 and not _P2NOMASK:
                B.wait("scalar", f"p2_mask{i}_{sc}")
            else:
                B.wait("scalar", f"p2_S{i}_{sc}")
            if i >= 1 and sc < 4 * i:
                B.wait("scalar", f"p2_Oc{i-1}_{sc}")
            q0 = 128 * (sc - (n - 4)) if sc >= n - 4 else 0
            B.mark("scalar", "act", 1, f"p2_exp{i}_{sc}",
                   lambda e, sc=sc, q0=q0, bank=pb[1 + sc % 2]: e.activation(
                       out=pt[:, sc * 512 + q0:(sc + 1) * 512], in_=bank[:, q0:512], func=AF.Exp, bias=cm16[:]))
        if _P2NOSUM:
            return
        B.wait("scalar", f"p2_sumdone{i}")
        B.mark("scalar", "act", 1, f"p2_sumev{i}",
               lambda e: e.activation(out=rows[0:1, RECIP:RECIP + 512], in_=pb[0][0:1, :], func=AF.Ln))
        B.op("scalar", lambda e: e.activation(out=rows[0:1, RECIP2:RECIP2 + 512],
                                              in_=rows[0:1, RECIP:RECIP + 512], func=AF.Exp, scale=-1.0))
        B.mark("scalar", "act", 1, f"p2_recipA{i}",
               lambda e: e.activation(out=rbf[0:1, SH:SH + 512], in_=rows[0:1, RECIP2:RECIP2 + 512], func=AF.Copy))
        B.wait("scalar", f"p2_rbmm{i}")
        if i >= 1:
            B.wait("scalar", f"p2_epidone{i-1}")
        B.mark("scalar", "act", 1, f"p2_rbev{i}",
               lambda e: e.activation(out=rb[:], in_=pb[3][:], func=AF.Copy))

    def p2_dve(i):
        n = 4 * (i + 1)
        B.wait("vector", "ld_masks")
        for k in range(4) if not _P2NOMASK else []:
            sc = n - 4 + k
            q0 = 128 * k
            B.wait("vector", f"p2_S{i}_{sc}")
            B.mark("vector", "dve", 1, f"p2_mask{i}_{sc}",
                   lambda e, k=k, q0=q0, bank=pb[1 + (n - 4 + k) % 2]: e.tensor_tensor(
                       out=bank[:, q0:q0 + 128], in0=bank[:, q0:q0 + 128],
                       in1=masks[:], op=ALU.add))
        if _P2NOSUM or _P2NOO:
            return
        B.wait("vector", f"p2_recipA{i}")
        B.mark("vector", "dve", 1, f"p2_sl{i}",
               lambda e: e.tensor_tensor(out=rbf[0:1, SL:SL + 512], in0=rows[0:1, RECIP2:RECIP2 + 512],
                                         in1=rbf[0:1, SH:SH + 512], op=ALU.subtract))
        if _P2NOEPI:
            return
        B.wait("vector", f"p2_x{i}")
        if not _P2NOSUM:
            B.wait("vector", f"p2_rbev{i}")
        EPIORD = [6, 7, 0, 1, 2, 3, 4, 5]

        def emit_mult(pos):
            dc = EPIORD[pos]
            B.wait("vector", f"p2_O{i}_{dc}")
            B.mark("vector", "dve", 1, f"p2_obank{i}_{dc}",
                  lambda e, dc=dc, pos=pos: e.tensor_tensor(
                      out=tmp32[:, (pos % 4) * 512:(pos % 4) * 512 + 512],
                      in0=pb[OPERM[dc]][:], in1=rb[:], op=ALU.mult))

        def emit_add(pos):
            dc = EPIORD[pos]
            if pos >= 2:
                B.wait("vector", f"p2_ybdma{i}_{EPIORD[pos-2]}")
            elif i >= 1:
                B.wait("vector", f"p2_ybdma{i-1}_{EPIORD[pos+6]}")
            B.mark("vector", "dve", 1, f"p2_epi{i}_{dc}",
                   lambda e, dc=dc, pos=pos, i=i: e.tensor_tensor(
                       out=ystage[:, (pos % 2) * 512:(pos % 2) * 512 + 512],
                       in0=tmp32[:, (pos % 4) * 512:(pos % 4) * 512 + 512],
                       in1=xbuf[i % 2][:, dc * 512:(dc + 1) * 512], op=ALU.add))

        for pos in range(4):
            emit_mult(pos)
        for pos in range(4, 8):
            emit_add(pos - 4)
            emit_mult(pos)
        for pos in range(4, 8):
            emit_add(pos)
        B.alias(f"p2_epidone{i}", f"p2_epi{i}_{EPIORD[-1]}")

    def p2_gp(i):
        if _P2NOO or _P2NOEPI:
            return
        for dc in [6, 7, 0, 1, 2, 3, 4, 5]:
            B.wait("gpsimd", f"p2_epi{i}_{dc}")
            B.mark("gpsimd", "st", 16, f"p2_ydma{i}_{dc}",
                   lambda e, i=i, dc=dc: e.dma_start(out=yscr.ap()[:, i, dc],
                                                     in_=ystage[:, (dc % 2) * 512:(dc % 2) * 512 + 512]))
            B.mark("gpsimd", "st", 16, f"p2_ybdma{i}_{dc}",
                   lambda e, i=i, dc=dc: e.dma_start(out=ybscr.ap()[:, i, dc],
                                                     in_=ystage[:, (dc % 2) * 512:(dc % 2) * 512 + 512]))
        B.alias(f"p2_ybdone{i}", f"p2_ybdma{i}_5")

    if _PHASES < 2:
        B.waitval("sync", "st", B.cnt["st"])
        return _finish(nc, B)
    B.waitval("tensor", "act", p1_act_end)
    B.waitval("tensor", "dve", p1_dve_end)
    B.waitval("scalar", "pe", p1_pe_end)
    for i in range(_P2N):
        p2_sync(i)
    for i in range(_P2N):
        p2_pe(i)
    for i in range(_P2N):
        p2_act(i)
    for i in range(_P2N):
        p2_dve(i)
    for i in range(_P2N):
        p2_gp(i)

    p2_act_end = B.cnt["act"]
    p2_dve_end = B.cnt["dve"]
    p2_pe_end = B.cnt["pe"]
    p2_st_end = B.cnt["st"]

    ybufs = [ybuf2, xbuf[0], xbuf[1]]

    def ybuf(i):
        return ybufs[i % 3]

    def p3_dve_pre(i):
        B.wait("vector", f"p3_xb{i}")
        if i >= 1:
            B.wait("vector", f"p3_stats{i-1}")
        B.mark("vector", "dve", 1, f"p3_sq{i}",
               lambda e: e.tensor_tensor(out=sq_ap(), in0=xb_ap(), in1=xb_ap(), op=ALU.mult))

    def p3_act_mrsb(i):
        B.wait("scalar", f"p3_bcast{i}")
        B.mark("scalar", "act", 1, f"p3_mrsb{i}",
               lambda e: e.activation(out=mrsb[:], in_=pb[3][:], func=AF.Copy))

    def p3_gp_yn1(i):
        if i >= 2:
            j = i - 2
            for dc in range(NDC):
                B.wait("gpsimd", f"p3_outadd{j}_{dc}")
                B.mark("gpsimd", "out", 16, f"p3_outdma{j}_{dc}",
                       lambda e, j=j, dc=dc: e.dma_start(out=out_h.ap()[j][:, dc],
                                                         in_=ystage[:, (dc % 2) * 512:(dc % 2) * 512 + 512]))
        B.wait("gpsimd", f"p3_mrsb{i}")
        B.wait("gpsimd", f"p3_y{i}")
        for dc in range(NDC):
            if dc >= 2:
                B.wait("gpsimd", f"p3_ynb{i}_{dc-2}")
            elif i >= 1:
                B.wait("gpsimd", f"p3_ynb{i-1}_{dc+6}")
            B.mark("gpsimd", "pool", 1, f"p3_yna{i}_{dc}",
                   lambda e, i=i, dc=dc: e.tensor_tensor(
                       out=ynt[:, (dc % 2) * 512:(dc % 2) * 512 + 512],
                       in0=ybuf(i)[:, dc * 512:(dc + 1) * 512], in1=mrsb[:], op=ALU.subtract))

    def p3_dve_yn2(i):
        if i >= 2:
            B.wait("vector", f"p3_P{i-2}_1")
        for dc in range(NDC):
            B.wait("vector", f"p3_yna{i}_{dc}")
            fn = lambda e, dc=dc, i=i: e.tensor_tensor(
                out=yn_ap(i)[:, dc * 512:(dc + 1) * 512],
                in0=ynt[:, (dc % 2) * 512:(dc % 2) * 512 + 512], in1=pb[2][:], op=ALU.mult)
            names = [f"p3_ynb{i}_{dc}"]
            if dc == NDC - 1:
                names.append(f"p3_yn{i}")
            B.mark("vector", "dve", 1, names, fn)

    def p3_pe_ffn(i):
        B.wait("tensor", f"p3_yn{i}")
        B.wait("tensor", "ffn_v2")
        if i >= 1:
            B.wait("tensor", f"p3_outadd{i-1}_5")
        for rc in range(2):
            for dc in range(NDC):
                fn = lambda e, rc=rc, dc=dc, i=i: e.matmul(
                    out=pb[4 + rc][:], lhsT=wv[:, dc * 256 + rc * 128: dc * 256 + rc * 128 + 128],
                    rhs=yn_ap(i)[:, dc * 512:(dc + 1) * 512],
                    start=(dc == 0), stop=(dc == NDC - 1))
                if dc == NDC - 1:
                    B.mark("tensor", "pe", 1, f"p3_P{i}_{rc}", fn)
                else:
                    B.op("tensor", fn)

        def q2_group(g):
            hfm = bf4 if g % 2 == 0 else bf5
            for fc in range(8 * g, 8 * g + 8):
                B.wait("tensor", f"p3_hev{i}_{fc}")
                for rc in range(2):
                    fn = lambda e, fc=fc, rc=rc, hfm=hfm: e.matmul(
                        out=pb[4 + rc][:], lhsT=wq[:, fc * 256 + rc * 128: fc * 256 + rc * 128 + 128],
                        rhs=hfm[:, (fc % 8) * 512:(fc % 8) * 512 + 512],
                        start=(fc == 0), stop=(fc == 31))
                    if fc % 8 == 7 and rc == 1:
                        names = [f"p3_q2g{i}_{g}"]
                        if g == 3:
                            names.append(f"p3_q2{i}")
                        B.mark("tensor", "pe", 1, names, fn)
                    else:
                        B.op("tensor", fn)

        B.wait("tensor", f"p3_pev{i}")
        HB = [6, 7, 0, 1]

        def emit_h(fc):
            if fc >= 4:
                B.wait("tensor", f"p3_hev{i}_{fc-4}")
            if fc % 4 == 2:
                if i + 2 < NST:
                    B.wait("tensor", f"p3_rowsA{i+2}")
                else:
                    B.wait("tensor", f"p3_rowsA{NST-1}")
            for rc in range(2):
                fn = lambda e, fc=fc, rc=rc: e.matmul(
                    out=pb[HB[fc % 4]][:], lhsT=wk[:, rc * 4096 + fc * 128: rc * 4096 + fc * 128 + 128],
                    rhs=p_sb[:, rc * 512:(rc + 1) * 512],
                    start=(rc == 0), stop=(rc == 1))
                if rc == 1:
                    B.mark("tensor", "pe", 1, f"p3_H{i}_{fc}", fn)
                else:
                    B.op("tensor", fn)

        def emit_q2(fc):
            g = fc // 8
            hfm = bf4 if g % 2 == 0 else bf5
            B.wait("tensor", f"p3_hev{i}_{fc}")
            for rc in range(2):
                fn = lambda e, fc=fc, rc=rc, hfm=hfm: e.matmul(
                    out=pb[4 + rc][:], lhsT=wq[:, fc * 256 + rc * 128: fc * 256 + rc * 128 + 128],
                    rhs=hfm[:, (fc % 8) * 512:(fc % 8) * 512 + 512],
                    start=(fc == 0), stop=(fc == 31))
                if fc % 8 == 7 and rc == 1:
                    names = [f"p3_q2g{i}_{fc // 8}"]
                    if fc == 31:
                        names.append(f"p3_q2{i}")
                    B.mark("tensor", "pe", 1, names, fn)
                else:
                    B.op("tensor", fn)

        for fc in range(12):
            emit_h(fc)
        for fc in range(12, 32):
            emit_h(fc)
            emit_q2(fc - 12)
        for fc in range(20, 32):
            emit_q2(fc)
        B.wait("tensor", f"p3_q2ev{i}")
        for dc in range(NDC):
            if dc >= 4:
                B.wait("tensor", f"p3_outadd{i}_{dc-4}")
            for rc in range(2):
                fn = lambda e, dc=dc, rc=rc: e.matmul(
                    out=pb[4 + dc % 4][:],
                    lhsT=wv[:, 2048 + rc * 1024 + dc * 128: 2048 + rc * 1024 + dc * 128 + 128],
                    rhs=q2[:, rc * 512:(rc + 1) * 512],
                    start=(rc == 0), stop=(rc == 1))
                if rc == 1:
                    B.mark("tensor", "pe", 1, f"p3_o2{i}_{dc}", fn)
                else:
                    B.op("tensor", fn)

    def p3_act_ffn(i):
        B.wait("scalar", f"p3_P{i}_0")
        B.op("scalar", lambda e: e.activation(out=p_sb[:, 0:512], in_=pb[4][:], func=AF.Copy))
        B.wait("scalar", f"p3_P{i}_1")
        B.mark("scalar", "act", 1, f"p3_pev{i}",
               lambda e: e.activation(out=p_sb[:, 512:1024], in_=pb[5][:], func=AF.Copy))

        def hev(fc):
            g = fc // 8
            B.wait("scalar", f"p3_H{i}_{fc}")
            if g >= 2:
                B.wait("scalar", f"p3_q2g{i}_{g-2}")
            elif i >= 1:
                B.wait("scalar", f"p3_q2g{i-1}_{g+2}")
            B.mark("scalar", "act", 1, f"p3_hev{i}_{fc}",
                   lambda e, fc=fc, hfm=(bf4 if (fc // 8) % 2 == 0 else bf5): e.activation(
                       out=hfm[:, (fc % 8) * 512:(fc % 8) * 512 + 512], in_=pb[[6, 7, 0, 1][fc % 4]][:],
                       func=AF.Gelu_apprx_tanh, bias=b1s[:, fc:fc + 1]))

        j = i + 2
        if j < _P3N:
            ln_rows_actA("p3", j)
        if i + 1 < _P3N:
            p3_act_mrsb(i + 1)
        for fc in range(0, 28):
            hev(fc)
        if j < _P3N:
            ln_rows_actB("p3", j)
        for fc in range(28, 32):
            hev(fc)
        B.wait("scalar", f"p3_q2{i}")
        B.op("scalar", lambda e: e.activation(out=q2[:, 0:512], in_=pb[4][:], func=AF.Copy))
        B.mark("scalar", "act", 1, f"p3_q2ev{i}",
               lambda e: e.activation(out=q2[:, 512:1024], in_=pb[5][:], func=AF.Copy))
        if j < _P3N:
            ln_rows_actC("p3", j)

    def p3_dve_out(i, dcs):
        for dc in dcs:
            B.wait("vector", f"p3_o2{i}_{dc}")
            if dc >= 2:
                B.wait("vector", f"p3_outdma{i}_{dc-2}")
            elif i >= 1:
                B.wait("vector", f"p3_outdma{i-1}_{dc+6}")
            B.mark("vector", "dve", 1, f"p3_outadd{i}_{dc}",
                   lambda e, i=i, dc=dc: e.scalar_tensor_tensor(
                       out=ystage[:, (dc % 2) * 512:(dc % 2) * 512 + 512],
                       in0=pb[4 + dc % 4][:], scalar=b2s[:, dc:dc + 1],
                       in1=ybuf(i)[:, dc * 512:(dc + 1) * 512],
                       op0=ALU.add, op1=ALU.add))

    def p3_sync(i):
        B.wait("sync", f"p2_ybdone{i}")
        if i >= 1:
            B.wait("sync", f"p3_stats{i-1}")
        B.mark("sync", "ld", 16, f"p3_xb{i}",
               lambda e, i=i: e.dma_start(out=pt[:, 0:4096].rearrange("p (c t) -> p c t", c=8),
                                          in_=ybscr.ap()[:, i]))
        B.wait("sync", f"p2_ydma{i}_5")
        if i == 1:
            B.wait("sync", f"p2_epidone{NST-2}")
        elif i == 2:
            B.wait("sync", f"p2_epidone{NST-1}")
        elif i >= 3:
            B.wait("sync", f"p3_outadd{i-3}_7")
        B.mark("sync", "ld", 16, f"p3_y{i}",
               lambda e, i=i: e.dma_start(out=ybuf(i)[:].rearrange("p (c t) -> p c t", c=8),
                                          in_=yscr.ap()[:, i]))

    if _PHASES < 3 or _P2N < NST:
        B.waitval("sync", "st", B.cnt["st"])
        return _finish(nc, B)
    B.waitval("tensor", "act", p2_act_end)
    B.waitval("tensor", "dve", p2_dve_end)
    B.waitval("scalar", "pe", p2_pe_end)
    B.waitval("vector", "st", p2_st_end)
    B.waitval("gpsimd", "dve", p2_dve_end)

    for i in range(_P3N):
        p3_sync(i)

    ln_rows_act("p3", 0)
    p3_act_mrsb(0)
    if _P3N > 1:
        ln_rows_act("p3", 1)
    for i in range(_P3N):
        p3_act_ffn(i)

    p3_dve_pre(0)
    ln_rows_dve("p3", 0)
    if _P3N > 1:
        p3_dve_pre(1)
        ln_rows_dve("p3", 1)
    p3_dve_yn2(0)
    if _P3N > 2:
        p3_dve_pre(2)
    for i in range(_P3N):
        if i + 3 < _P3N:
            p3_dve_pre(i + 3)
        if i + 2 < _P3N:
            ln_rows_dve_var("p3", i + 2)
        if i + 1 < _P3N:
            p3_dve_yn2(i + 1)
        p3_dve_out(i, range(0, 4))
        if i + 2 < _P3N:
            ln_rows_dve_mr("p3", i + 2)
            ln_rows_dve_ml("p3", i + 2)
        p3_dve_out(i, range(4, NDC))

    for i in range(_P3N):
        p3_gp_yn1(i)
    for j in range(max(0, _P3N - 2), _P3N):
        for dc in range(NDC):
            B.wait("gpsimd", f"p3_outadd{j}_{dc}")
            B.mark("gpsimd", "out", 16, f"p3_outdma{j}_{dc}",
                   lambda e, j=j, dc=dc: e.dma_start(out=out_h.ap()[j][:, dc],
                                                     in_=ystage[:, (dc % 2) * 512:(dc % 2) * 512 + 512]))
    B.waitval("gpsimd", "out", B.cnt["out"])

    pe_stats("p3", 0, xb_ap, sq_ap)
    pe_bcast("p3", 0, f"p2_epidone{NST-1}")
    if _P3N > 1:
        pe_stats("p3", 1, xb_ap, sq_ap)
    for i in range(_P3N):
        if i + 1 < _P3N:
            pe_bcast("p3", i + 1, f"p3_yn{i}")
        if i + 2 < _P3N:
            pe_stats("p3", i + 2, xb_ap, sq_ap)
        p3_pe_ffn(i)

    return _finish(nc, B)


def _finish(nc, B):
    final = B.finalize()
    with nc.Block() as blk:
        @blk.sync
        def _(e):
            for fn in final["sync"]:
                fn(e)

        @blk.gpsimd
        def _(e):
            for fn in final["gpsimd"]:
                fn(e)

        @blk.scalar
        def _(e):
            for fn in final["scalar"]:
                fn(e)

        @blk.vector
        def _(e):
            for fn in final["vector"]:
                fn(e)

        @blk.tensor
        def _(e):
            for fn in final["tensor"]:
                fn(e)

    return nc


_NC_CACHE = None


def _get_nc():
    global _NC_CACHE
    if _NC_CACHE is None:
        _NC_CACHE = build_nc()
    return _NC_CACHE


def kernel(L0_attn_x, L0_attn_Wq, L0_attn_Wk, L0_attn_Wv,
           L0_ffn_U1, L0_ffn_V1, L0_ffn_b1, L0_ffn_U2, L0_ffn_V2, L0_ffn_b2):
    x = np.asarray(L0_attn_x, dtype=np.float32)
    Bsz, T, D = x.shape
    assert (Bsz, T, D) == (8, 4096, 1024)

    def wchunk(w):
        w = np.asarray(w, dtype=np.float32)
        nch = w.shape[0] // 128
        return np.ascontiguousarray(w.reshape(nch, 128, w.shape[1]).transpose(1, 0, 2))

    shared = dict(
        wq_h=wchunk(L0_attn_Wq), wk_h=wchunk(L0_attn_Wk), wv_h=wchunk(L0_attn_Wv),
        u1_h=wchunk(L0_ffn_U1), v1_h=wchunk(L0_ffn_V1),
        u2_h=wchunk(L0_ffn_U2), v2_h=wchunk(L0_ffn_V2),
        b1_h=np.ascontiguousarray(np.asarray(L0_ffn_b1, dtype=np.float32).reshape(32, 128).T),
        b2_h=np.ascontiguousarray(np.asarray(L0_ffn_b2, dtype=np.float32).reshape(8, 128).T),
    )
    p_idx = np.arange(128)[:, None]
    j_idx = np.arange(128)[None, :]
    shared["mk_h"] = np.where(j_idx >= p_idx, np.float32(0.0), np.float32(NEG)).astype(BFNP)

    in_maps = []
    for b in range(Bsz):
        xt = np.ascontiguousarray(x[b].T.reshape(NDC, 128, NST, ST).transpose(2, 1, 0, 3))
        in_maps.append(dict(xin=xt, **shared))

    nc = _get_nc()
    res = run_bass_kernel_spmd(nc, in_maps, core_ids=list(range(8)))

    outs = []
    for b in range(Bsz):
        o = np.asarray(res.results[b]["out_h"], dtype=np.float32)
        outs.append(o.transpose(0, 3, 2, 1).reshape(T, D))
    full = np.stack(outs)
    if _DEBUG:
        kernel._debug_results = res
    return full
